# version 25
# baseline (speedup 1.0000x reference)
import numpy as np
import ml_dtypes
from contextlib import ExitStack

import concourse.bass as bass
import concourse.mybir as mybir
from concourse.bass_utils import run_bass_kernel_spmd

F32 = mybir.dt.float32
BF16 = mybir.dt.bfloat16
AF = mybir.ActivationFunctionType
ALU = mybir.AluOpType
AX = mybir.AxisListType
BFNP = ml_dtypes.bfloat16

T = 2048
D = 1024
NTC = 16
NNT = 4
QSCALE = 0.17677669529663687
JORDER = [0, 4, 1, 5, 2, 6, 3, 7]

ENGSEM = {"tensor": "t", "vector": "v", "scalar": "s", "gpsimd": "g", "sync": "y"}


class Res:
    __slots__ = ("w", "rs", "name")
    def __init__(self, name=""):
        self.w = []
        self.rs = []
        self.name = name


class Prog:
    def __init__(self, nc, es):
        self.nc = nc
        self.q = {e: [] for e in ENGSEM}
        self.cnt = {}
        self.waited = {e: {} for e in ENGSEM}
        self.fence_pend = {e: {} for e in ENGSEM}
        self.sems = {}
        self.es = es

    def sem(self, name):
        if name not in self.sems:
            self.sems[name] = self.es.enter_context(self.nc.semaphore(name))
            self.cnt[name] = 0
        return self.sems[name]

    def op(self, engine, builders, rd=(), wr=(), dma=None):
        if callable(builders):
            builders = [builders]
        deps = {}
        def add(tok):
            s, v = tok
            if deps.get(s, 0) < v:
                deps[s] = v
        for r in rd:
            for tk in r.w:
                add(tk)
        for w in wr:
            for tk in w.w:
                add(tk)
            for tk in w.rs:
                add(tk)
        for s, v in self.fence_pend[engine].items():
            if deps.get(s, 0) < v:
                deps[s] = v
        self.fence_pend[engine] = {}
        waits = []
        wt = self.waited[engine]
        for s, v in sorted(deps.items()):
            if v > wt.get(s, 0):
                waits.append((s, v))
                wt[s] = v
        if dma is not None:
            semname, amt = dma, 16
        else:
            semname, amt = ENGSEM[engine], 1
        self.sem(semname)
        self.cnt[semname] += amt
        tok = (semname, self.cnt[semname])
        self.q[engine].append((waits, builders, semname, amt))
        for r in rd:
            r.rs.append(tok)
        for w in wr:
            if w.rs:
                w.w = [tok]
                w.rs = []
            else:
                w.w.append(tok)
        return tok

    def fence(self, engines=None):
        snap = dict(self.cnt)
        for e in (engines or ENGSEM):
            fp = self.fence_pend[e]
            for s, v in snap.items():
                if fp.get(s, 0) < v:
                    fp[s] = v

    def replay(self, block):
        handles = {"tensor": block.tensor, "vector": block.vector,
                   "scalar": block.scalar, "gpsimd": block.gpsimd,
                   "sync": block.sync}
        for ename, deco in handles.items():
            queue = self.q[ename]
            if not queue:
                continue
            sems = self.sems
            def body(eng, queue=list(queue), sems=sems):
                for waits, builders, semname, amt in queue:
                    for s, v in waits:
                        eng.wait_ge(sems[s], v)
                    for b in builders[:-1]:
                        b()
                    builders[-1]().then_inc(sems[semname], amt)
            deco(body)
            self.q[ename] = []


def _pack_vec(v, chunks):
    return np.ascontiguousarray(np.asarray(v, np.float32).reshape(chunks, 128).T)


def _bias_tile(rpb, w, G):
    n = G * w
    coords = np.arange(w)
    ridx = coords[:, None] - coords[None, :] + w - 1
    out = np.full((8, n, n), -30000.0, np.float32)
    for g in range(G):
        sl = slice(g * w, (g + 1) * w)
        for h in range(8):
            out[h][sl, sl] = np.asarray(rpb, np.float32)[ridx, h]
    return np.ascontiguousarray(np.concatenate([out[h] for h in JORDER], axis=1))


def _special_tile(rpb, w, idx):
    idx = np.asarray(idx)
    ridx = idx[:, None] - idx[None, :] + w - 1
    out = np.stack([np.asarray(rpb, np.float32)[ridx, h] for h in range(8)], 0)
    return np.ascontiguousarray(np.concatenate([out[h] for h in JORDER], axis=1))


def build_consts(inp):
    c = {}
    c["idb"] = np.eye(128, dtype=BFNP)
    c["epsc"] = np.ascontiguousarray(
        np.broadcast_to(np.array([1e-5, 1e-6], np.float32), (128, 2)))
    c["idf"] = np.eye(128, dtype=np.float32)
    c["onesb"] = np.ones((128, 128), dtype=BFNP)
    c["onesf"] = np.ones((1, 128), dtype=np.float32)
    c["bt1"] = _bias_tile(inp["L0_attn_rpb1"], 10, 12)
    c["bt2"] = _bias_tile(inp["L0_attn_rpb2"], 20, 6)
    c["bs1"] = _special_tile(inp["L0_attn_rpb1"], 10, np.arange(10))
    c["bs2"] = _special_tile(inp["L0_attn_rpb2"], 20, np.arange(10, 20))
    lab = np.array([1, 1, 1, 1, 1, 2, 2, 2, 2, 2], np.float32)
    am = np.where(lab[:, None] != lab[None, :], -100.0, 0.0).astype(np.float32)
    c["am1"] = np.ascontiguousarray(np.tile(am, (1, 8)))
    c["vb1"] = np.ascontiguousarray(
        np.broadcast_to(np.asarray(inp["L0_attn_qkv1_b"], np.float32)[512:768], (128, 256)))
    c["vb2"] = np.ascontiguousarray(
        np.broadcast_to(np.asarray(inp["L0_attn_qkv2_b"], np.float32)[512:768], (128, 256)))
    c["dbc"] = _pack_vec(inp["L0_attn_down_b"], 4)
    c["n1c"] = _pack_vec(inp["L0_attn_norm1_w"], 4)
    c["n2c"] = _pack_vec(inp["L0_attn_norm2_w"], 8)
    c["ubc"] = _pack_vec(inp["L0_attn_up_b"], 8)
    c["qb1"] = _pack_vec(np.asarray(inp["L0_attn_qkv1_b"], np.float32)[:512], 4)
    c["qb2"] = _pack_vec(np.asarray(inp["L0_attn_qkv2_b"], np.float32)[:512], 4)
    c["pb1"] = _pack_vec(inp["L0_attn_proj1_b"], 2)
    c["pb2"] = _pack_vec(inp["L0_attn_proj2_b"], 2)
    c["fb1c"] = _pack_vec(inp["L0_attn_ffn_b1"], 16)
    c["rb1c"] = _pack_vec(inp["L0_ffn_rb1"], 8)
    c["rb2c"] = np.ascontiguousarray(np.asarray(inp["L0_ffn_rb2"], np.float32).reshape(8, 1))
    eb1 = np.asarray(inp["L0_ffn_eb1"], np.float32)
    c["eb1c"] = np.ascontiguousarray(
        np.concatenate([_pack_vec(eb1[e], 16) for e in range(8)], axis=1))
    c["fb2"] = np.ascontiguousarray(np.asarray(inp["L0_attn_ffn_b2"], np.float32).reshape(1, 1024))
    c["eb2"] = np.ascontiguousarray(np.asarray(inp["L0_ffn_eb2"], np.float32))
    for n, k in [("dw", "L0_attn_down_w"), ("q1w", "L0_attn_qkv1_w"),
                 ("q2w", "L0_attn_qkv2_w"), ("p1w", "L0_attn_proj1_w"),
                 ("p2w", "L0_attn_proj2_w"), ("uw", "L0_attn_up_w"),
                 ("f1w", "L0_attn_ffn_w1"), ("f2w", "L0_attn_ffn_w2"),
                 ("r1w", "L0_ffn_rw1"), ("r2w", "L0_ffn_rw2"),
                 ("e1w", "L0_ffn_ew1"), ("e2w", "L0_ffn_ew2")]:
        c[n] = np.ascontiguousarray(np.asarray(inp[k], np.float32))
    return c


CSHAPES = {
    "idb": ([128, 128], BF16), "idf": ([128, 128], F32),
    "epsc": ([128, 2], F32),
    "onesb": ([128, 128], BF16), "onesf": ([1, 128], F32),
    "bt1": ([120, 960], F32), "bt2": ([120, 960], F32),
    "bs1": ([10, 80], F32), "bs2": ([10, 80], F32), "am1": ([10, 80], F32),
    "vb1": ([128, 256], F32), "vb2": ([128, 256], F32),
    "dbc": ([128, 4], F32), "n1c": ([128, 4], F32), "n2c": ([128, 8], F32),
    "ubc": ([128, 8], F32), "qb1": ([128, 4], F32), "qb2": ([128, 4], F32),
    "pb1": ([128, 2], F32), "pb2": ([128, 2], F32), "fb1c": ([128, 16], F32),
    "rb1c": ([128, 8], F32), "rb2c": ([8, 1], F32), "eb1c": ([128, 128], F32),
}
WSHAPES = {
    "fb2": ([1, 1024], F32), "eb2": ([8, 1024], F32),
    "dw": ([1024, 512], F32), "q1w": ([256, 768], F32), "q2w": ([256, 768], F32),
    "p1w": ([256, 256], F32), "p2w": ([256, 256], F32), "uw": ([512, 1024], F32),
    "f1w": ([1024, 2048], F32), "f2w": ([2048, 1024], F32),
    "r1w": ([1024, 1024], F32), "r2w": ([1024, 8], F32),
    "e1w": ([8, 1024, 2048], F32), "e2w": ([8, 2048, 1024], F32),
}


def build_nc():
    nc = bass.Bass("TRN2", target_bir_lowering=False)
    es = ExitStack()
    P = Prog(nc, es)
    t_, v_, s_, g_, y_ = nc.tensor, nc.vector, nc.scalar, nc.gpsimd, nc.sync

    xd = nc.dram_tensor("x", [T, D], F32, kind="ExternalInput")
    outd = nc.dram_tensor("out", [T, D], F32, kind="ExternalOutput")
    CN = {}
    for n, (shp, dt) in {**CSHAPES, **WSHAPES}.items():
        CN[n] = nc.dram_tensor(n, shp, dt, kind="ExternalInput")

    def sb(name, shape, dt):
        return es.enter_context(nc.sbuf_tensor(name, shape, dt))

    S = sb("S", [128, 16 * 1024], BF16)
    XN = sb("XN", [128, 8 * 2048], BF16)
    WTM = sb("WTM", [128, 16 * 8], F32)
    LTM = sb("LTM", [128, 16 * 8], F32)
    WFM = sb("WFM", [8, 2048], BF16)

    MVS = sb("MVS", [128, 16], F32)

    QBS = sb("QBS", [128, 4], F32)
    BRANCH_CONSTS = ("bt1", "bt2", "bs1", "bs2", "am1", "vb1", "vb2")
    CS = {n: sb("c_" + n, shp, dt) for n, (shp, dt) in CSHAPES.items()
          if n not in BRANCH_CONSTS}
    PF = es.enter_context(nc.psum_tensor("PF", [128, 3072], F32))
    PBF = es.enter_context(nc.psum_tensor("PBF", [128, 2048], BF16))
    rPF = [Res(f"pf{i}") for i in range(6)]
    rPBF = [Res("pbf0"), Res("pbf1")]
    def Fb(i):
        return PF[:, 512 * i: 512 * i + 512]
    FB2R = sb("FB2R", [1, 1024], BF16)

    rS = [Res(f"S{i}") for i in range(NTC)]
    rXN4 = [Res(f"XN{i}") for i in range(4)]
    rWTM, rLTM, rWFM = Res("WTM"), Res("LTM"), Res("WFM")
    rGAT, rMVS, rRSTD, rQBS = Res("GAT"), Res("MVS"), Res("RSTD"), Res("QBS")
    rCS = {n: Res(n) for n in CS}
    rFB2R, rEB2R = Res("FB2R"), Res("EB2R")

    XN3 = XN[:, :].rearrange("p (c x) -> p c x", c=8)

    def wload(dst_ap, src_ap, res, semname):
        P.op("gpsimd", (lambda d=dst_ap, a=src_ap: g_.dma_start(d, a)),
             wr=[res], dma=semname)

    WDOWN = sb("WDOWN", [128, 8 * 512], BF16)
    rWDOWN = Res("WD")
    wload(WDOWN[:, :].rearrange("p (k m) -> p k m", k=8),
          CN["dw"].ap().rearrange("(k p) m -> p k m", p=128), rWDOWN, "dwdn")
    rWR1g, rWR2g = Res("WR1"), Res("WR2")

    blk_cm = nc.Block()
    blk = blk_cm.__enter__()

    def emit_const_loads():
        for n in CS:
            P.op("sync", (lambda d=CS[n], a=CN[n].ap(): y_.dma_start(d[:, :], a)),
                 wr=[rCS[n]], dma="dc_" + n)
    for n in BRANCH_CONSTS:
        rCS[n] = Res(n)
    P.op("gpsimd", (lambda: g_.dma_start(FB2R[:, :], CN["fb2"].ap())),
         wr=[rFB2R], dma="dcg1")

    esB = ExitStack()
    GAT = esB.enter_context(nc.sbuf_tensor("GAT", [128, 4 * 32], F32))
    GA2 = esB.enter_context(nc.sbuf_tensor("GA2", [128, 32], F32))
    GB2 = esB.enter_context(nc.sbuf_tensor("GB2", [128, 4 * 32], F32))
    GC2 = esB.enter_context(nc.sbuf_tensor("GC2", [128, 4 * 32], F32))
    WR1 = esB.enter_context(nc.sbuf_tensor("WR1", [128, 8 * 1024], BF16))
    WR2 = esB.enter_context(nc.sbuf_tensor("WR2", [128, 8 * 8], BF16))
    wload(WR1[:, :].rearrange("p (k m) -> p k m", k=8),
          CN["r1w"].ap().rearrange("(k p) m -> p k m", p=128), rWR1g, "dwr1")
    wload(WR2[:, :].rearrange("p (k m) -> p k m", k=8),
          CN["r2w"].ap().rearrange("(k p) m -> p k m", p=128), rWR2g, "dwr2")
    G1 = esB.enter_context(nc.sbuf_tensor("G1", [128, 8 * 2048], BF16))
    LOG = esB.enter_context(nc.sbuf_tensor("LOG", [8, 2048], F32))

    es0 = ExitStack()
    PT = PBF
    XST = es0.enter_context(nc.sbuf_tensor("XST", [128, 4 * 1024], F32))
    XNT = es0.enter_context(nc.sbuf_tensor("XNT", [128, 4 * 1024], BF16))
    MV4 = es0.enter_context(nc.sbuf_tensor("MV4", [128, 4 * 16], F32))
    if True:
        rPT = rPBF
        rXST = [Res(f"XST{i}") for i in range(4)]
        rXNT = [Res(f"XNT{i}") for i in range(4)]
        rMV4 = [Res(f"MV4{i}") for i in range(4)]

        def p0_stages(tc):
            p4 = tc % 4
            p2 = tc % 2
            xs = XST[:, 1024 * p4: 1024 * p4 + 1024]
            xnt = XNT[:, 1024 * p4: 1024 * p4 + 1024]
            mv = MV4[:, 16 * p4: 16 * p4 + 16]
            rmv = rMV4[p4]

            def u0():
                P.op("sync", (lambda tc=tc, xs=xs: y_.dma_start(
                    xs, xd.ap()[128 * tc: 128 * tc + 128, :])),
                    wr=[rXST[p4]], dma=f"dx{p4}")

            def u1():
                P.op("vector", (lambda: v_.bn_stats(mv[:, 0:6], xs[:, 0:512])),
                     rd=[rXST[p4]], wr=[rmv])
                P.op("vector", (lambda: v_.bn_stats(mv[:, 6:12], xs[:, 512:1024])),
                     rd=[rXST[p4]], wr=[rmv])
                P.op("vector", (lambda: v_.bn_aggr(mv[:, 12:14], mv[:, 0:12])),
                     rd=[rmv], wr=[rmv])

            def u2():
                P.op("scalar", (lambda: s_.activation(mv[:, 14:15], mv[:, 13:14],
                                                      AF.Ln, bias=CS["epsc"][:, 0:1])),
                     rd=[rmv, rCS["epsc"]], wr=[rmv])
                P.op("scalar", (lambda: s_.activation(mv[:, 15:16], mv[:, 14:15],
                                                      AF.Exp, scale=-0.5)),
                     rd=[rmv], wr=[rmv])
            def u3():
                P.op("vector", (lambda: v_.tensor_scalar(
                    xnt, xs, mv[:, 12:13], mv[:, 15:16], ALU.subtract, ALU.mult)),
                    rd=[rXST[p4], rmv], wr=[rXNT[p4]])
                P.op("vector", (lambda tc=tc: v_.tensor_tensor(
                    S[:, 1024 * tc: 1024 * tc + 1024], xs, xnt, ALU.add)),
                    rd=[rXST[p4], rXNT[p4]], wr=[rS[tc]])

            def u4():
                for fc in range(8):
                    P.op("tensor", (lambda fc=fc: t_.transpose(
                        PT[:, 1024 * p2 + 128 * fc: 1024 * p2 + 128 * fc + 128],
                        xnt[:, 128 * fc: 128 * fc + 128], CS["idb"][:, :])),
                        rd=[rXNT[p4], rCS["idb"]], wr=[rPT[p2]])

            def u5():
                P.op("scalar", (lambda tc=tc: s_.copy(
                    XN3[:, :, 128 * tc: 128 * tc + 128],
                    PT[:, 1024 * p2: 1024 * p2 + 1024].rearrange(
                        "p (c x) -> p c x", c=8))),
                    rd=[rPT[p2]], wr=[rXN4[tc // 4]])

            return [u0, u1, u2, u3, u4, u5]

        p0s = [p0_stages(tc) for tc in range(NTC)]
        for tc in range(4):
            p0s[tc][0]()
        emit_const_loads()
        P.op("scalar", (lambda: s_.mul(QBS[:, 0:2], CS["qb1"][:, 0:2], QSCALE)),
             rd=[rCS["qb1"]], wr=[rQBS])
        P.op("scalar", (lambda: s_.mul(QBS[:, 2:4], CS["qb2"][:, 0:2], QSCALE)),
             rd=[rCS["qb2"]], wr=[rQBS])
        for step in range(NTC + 5):
            for sidx in range(6):
                t = step - sidx
                if 0 <= t < NTC and not (sidx == 0 and t < 4):
                    p0s[t][sidx]()
        P.fence()
        P.replay(blk)
    es0.close()

    PA, PB, PS, PD = Fb(0), Fb(1), Fb(2), Fb(3)
    if True:
        rPA, rPB, rPS, rPD = rPF[0], rPF[1], rPF[2], rPF[3]
        rWR1, rWR2, rG1, rLOG = rWR1g, rWR2g, Res("G1"), Res("LOG")
        pbanks = [(PA, rPA), (PB, rPB)]
        for m in range(8):
            for nt in range(NNT):
                ps, rps = pbanks[(m * NNT + nt) % 2]
                blds = [
                    (lambda m=m, nt=nt, k=k, ps=ps: t_.matmul(
                        ps[:, 0:512],
                        WR1[:, 1024 * k + 128 * m: 1024 * k + 128 * m + 128],
                        XN3[:, k, 512 * nt: 512 * nt + 512],
                        start=(k == 0), stop=(k == 7)))
                    for k in range(8)]
                P.op("tensor", blds, rd=[rWR1, rXN4[nt]], wr=[rps])
                P.op("scalar", (lambda m=m, nt=nt, ps=ps: s_.activation(
                    G1[:, 2048 * m + 512 * nt: 2048 * m + 512 * nt + 512],
                    ps[:, 0:512], AF.Silu, bias=CS["rb1c"][:, m: m + 1])),
                    rd=[rps, rCS["rb1c"]], wr=[rG1])
        G13 = G1[:, :].rearrange("p (c x) -> p c x", c=8)
        for nt in range(NNT):
            blds = [
                (lambda nt=nt, k=k: t_.matmul(
                    PS[0:8, 0:512], WR2[:, 8 * k: 8 * k + 8],
                    G13[:, k, 512 * nt: 512 * nt + 512],
                    start=(k == 0), stop=(k == 7)))
                for k in range(8)]
            P.op("tensor", blds, rd=[rWR2, rG1], wr=[rPS])
            P.op("scalar", (lambda nt=nt: s_.activation(
                LOG[0:8, 512 * nt: 512 * nt + 512], PS[0:8, 0:512],
                AF.Identity, bias=CS["rb2c"][0:8, 0:1])),
                rd=[rPS, rCS["rb2c"]], wr=[rLOG])
        rGT = Res("GT")
        PDB = [PA, PB, PS, PD]
        rPD4 = [rPF[0], rPF[1], rPF[2], rPF[3]]
        for tc in range(NTC):
            p4 = tc % 4
            pd = PDB[p4][:, 0:128]
            P.op("tensor", (lambda tc=tc, pd=pd: t_.transpose(
                pd[:, 0:8], LOG[0:8, 128 * tc: 128 * tc + 128],
                CS["idf"][0:8, 0:8])),
                rd=[rLOG, rCS["idf"]], wr=[rPD4[p4]])
            P.op("vector", (lambda tc=tc, pd=pd: v_.tensor_copy(
                LTM[:, 8 * tc: 8 * tc + 8], pd[:, 0:8])),
                rd=[rPD4[p4]], wr=[rLTM])
        for q in range(4):
            lt = LTM[:, 32 * q: 32 * q + 32]
            lt3 = lt.rearrange("p (c j) -> p c j", c=4)
            P.op("vector", (lambda lt3=lt3, q=q: v_.reduce_max(
                GA2[:, 8 * q: 8 * q + 4], lt3, axis=AX.X)),
                rd=[rLTM], wr=[rGT])
            P.op("vector", (lambda lt3=lt3, q=q: v_.reduce_max(
                GA2[:, 8 * q + 4: 8 * q + 8], lt3, axis=AX.X, negate=True)),
                rd=[rLTM], wr=[rGT])
            negx = bass.AP(GA2, 8 * q + 4, [[32, 128], [1, 4], [0, 8]])
            m1xx = bass.AP(GA2, 8 * q, [[32, 128], [1, 4], [0, 8]])
            P.op("vector", (lambda lt3=lt3, negx=negx, q=q: v_.tensor_tensor(
                GB2[:, 32 * q: 32 * q + 32].rearrange("p (c j) -> p c j", c=4),
                lt3, negx, ALU.add)),
                rd=[rLTM, rGT], wr=[rGT])
            P.op("scalar", (lambda q=q: s_.activation(
                GAT[:, 32 * q: 32 * q + 32], GB2[:, 32 * q: 32 * q + 32],
                AF.Exp)),
                rd=[rGT], wr=[rGT])
            P.op("vector", (lambda lt3=lt3, m1xx=m1xx, q=q: v_.tensor_tensor(
                GC2[:, 32 * q: 32 * q + 32].rearrange("p (c j) -> p c j", c=4),
                lt3, m1xx, ALU.is_ge)),
                rd=[rLTM, rGT], wr=[rGT])
            P.op("vector", (lambda lt=lt, q=q: v_.scalar_tensor_tensor(
                GC2[:, 32 * q: 32 * q + 32], GC2[:, 32 * q: 32 * q + 32],
                -1e9, lt, ALU.mult, ALU.add)),
                rd=[rLTM, rGT], wr=[rGT])
            P.op("vector", (lambda q=q: v_.reduce_max(
                GA2[:, 8 * q: 8 * q + 4],
                GC2[:, 32 * q: 32 * q + 32].rearrange("p (c j) -> p c j", c=4),
                axis=AX.X)),
                rd=[rGT], wr=[rGT])
            P.op("vector", (lambda lt3=lt3, m1xx=m1xx, q=q: v_.tensor_tensor(
                GC2[:, 32 * q: 32 * q + 32].rearrange("p (c j) -> p c j", c=4),
                lt3, m1xx, ALU.is_ge)),
                rd=[rLTM, rGT], wr=[rGT])
            P.op("vector", (lambda q=q: v_.tensor_tensor(
                GAT[:, 32 * q: 32 * q + 32], GAT[:, 32 * q: 32 * q + 32],
                GC2[:, 32 * q: 32 * q + 32], ALU.mult)),
                rd=[rGT], wr=[rGT])
            P.op("vector", (lambda q=q: v_.reduce_sum(
                GA2[:, 8 * q: 8 * q + 4],
                GAT[:, 32 * q: 32 * q + 32].rearrange("p (c j) -> p c j", c=4),
                axis=AX.X)),
                rd=[rGT], wr=[rGT])
            P.op("vector", (lambda q=q: v_.reciprocal(
                GA2[:, 8 * q + 4: 8 * q + 8], GA2[:, 8 * q: 8 * q + 4])),
                rd=[rGT], wr=[rGT])
            rx = bass.AP(GA2, 8 * q + 4, [[32, 128], [1, 4], [0, 8]])
            P.op("vector", (lambda rx=rx, q=q: v_.tensor_tensor(
                WTM[:, 32 * q: 32 * q + 32].rearrange("p (c j) -> p c j", c=4),
                GAT[:, 32 * q: 32 * q + 32].rearrange("p (c j) -> p c j", c=4),
                rx, ALU.mult)),
                rd=[rGT], wr=[rWTM])
            for tc in range(4 * q, 4 * q + 4):
                p4 = tc % 4
                pd = PDB[p4][:, 0:128]
                P.op("tensor", (lambda tc=tc, pd=pd: t_.transpose(
                    pd[0:8, 0:128], WTM[:, 8 * tc: 8 * tc + 8], CS["idf"][:, :])),
                    rd=[rWTM, rCS["idf"]], wr=[rPD4[p4]])
                P.op("scalar", (lambda tc=tc, pd=pd: s_.copy(
                    WFM[0:8, 128 * tc: 128 * tc + 128], pd[0:8, 0:128])),
                    rd=[rPD4[p4]], wr=[rWFM])

        P.fence()
        P.replay(blk)
    esB.close()

    esU = ExitStack()
    UPIN = esU.enter_context(nc.sbuf_tensor("UPIN", [128, 4 * 2048], BF16))
    WUP = esU.enter_context(nc.sbuf_tensor("WUP", [128, 4 * 1024], BF16))
    if True:
        rUPIN = [Res(f"UPIN{i}") for i in range(NNT)]
        rWUP = Res("WUP")
        wload(WUP[:, :].rearrange("p (k m) -> p k m", k=4),
              CN["uw"].ap().rearrange("(k p) m -> p k m", p=128), rWUP, "dwup")
        UPIN3 = UPIN[:, :].rearrange("p (c x) -> p c x", c=4)

        def upin_res(lo, hi):
            return [rUPIN[i] for i in range(NNT) if lo < 512 * (i + 1) and hi > 512 * i]

        esA = ExitStack()
        XI = esA.enter_context(nc.sbuf_tensor("XI", [128, 4 * 2052], BF16))
        QKV = esA.enter_context(nc.sbuf_tensor("QKV", [128, 4 * 2052], BF16))
        VT17 = esA.enter_context(nc.sbuf_tensor("VT17", [128, 17 * 256], BF16))
        Z = esA.enter_context(nc.sbuf_tensor("Z", [128, 4 * 960], F32))
        E = esA.enter_context(nc.sbuf_tensor("E", [128, 4 * 960], BF16))
        PTS = esA.enter_context(nc.sbuf_tensor("PTS", [128, 2 * 1024], BF16))
        AVS = esA.enter_context(nc.sbuf_tensor("AVS", [128, 2 * 240], BF16))
        NEG = esA.enter_context(nc.sbuf_tensor("NEG", [128, 4 * 8], F32))
        SUM = esA.enter_context(nc.sbuf_tensor("SUMS", [128, 4 * 8], F32))
        RCP = esA.enter_context(nc.sbuf_tensor("RCPS", [128, 4 * 8], F32))
        SQ = esA.enter_context(nc.sbuf_tensor("SQ", [128, 4 * 512], BF16))
        XS = esA.enter_context(nc.sbuf_tensor("XS", [128, 2 * 16], BF16))
        QKVS = esA.enter_context(nc.sbuf_tensor("QKVS", [128, 4 * 16], BF16))
        VS = esA.enter_context(nc.sbuf_tensor("VS", [128, 256], BF16))
        RSTD = esA.enter_context(nc.sbuf_tensor("RSTD", [1, 512], F32))
        WQ1 = esA.enter_context(nc.sbuf_tensor("WQ1", [128, 2 * 768], BF16))
        WQ2 = esA.enter_context(nc.sbuf_tensor("WQ2", [128, 2 * 768], BF16))
        WPJ1 = esA.enter_context(nc.sbuf_tensor("WPJ1", [128, 2 * 256], BF16))
        WPJ2 = esA.enter_context(nc.sbuf_tensor("WPJ2", [128, 2 * 256], BF16))
        for n in BRANCH_CONSTS:
            shp, dt = CSHAPES[n]
            CS[n] = esA.enter_context(nc.sbuf_tensor("c_" + n, shp, dt))
            P.op("sync", (lambda d=CS[n], a=CN[n].ap(): y_.dma_start(d[:, :], a)),
                 wr=[rCS[n]], dma="dc_" + n)
        if True:
            rXI, rQKV = Res("XI"), Res("QKV")
            rVT17 = Res("VT17")
            rZ = [Res(f"Z{i}") for i in range(4)]
            rE = [Res(f"E{i}") for i in range(4)]
            rPTS = [Res("PTS0"), Res("PTS1")]
            rAVS = [Res("AVS0"), Res("AVS1")]
            rNEGL = [Res(f"NEG{i}") for i in range(4)]
            rSUML = [Res(f"SUM{i}") for i in range(4)]
            rRCPL = [Res(f"RCP{i}") for i in range(4)]
            rSQ = Res("SQ")
            rXS, rQKVS, rVS = Res("XS"), Res("QKVS"), Res("VS")
            rWQ1, rWQ2, rWPJ1, rWPJ2 = Res("WQ1"), Res("WQ2"), Res("WP1"), Res("WP2")
            wload(WQ1[:, :].rearrange("p (k m) -> p k m", k=2),
                  CN["q1w"].ap().rearrange("(k p) m -> p k m", p=128), rWQ1, "dwq1")
            wload(WQ2[:, :].rearrange("p (k m) -> p k m", k=2),
                  CN["q2w"].ap().rearrange("(k p) m -> p k m", p=128), rWQ2, "dwq2")
            wload(WPJ1[:, :].rearrange("p (k m) -> p k m", k=2),
                  CN["p1w"].ap().rearrange("(k p) m -> p k m", p=128), rWPJ1, "dwp1")
            wload(WPJ2[:, :].rearrange("p (k m) -> p k m", k=2),
                  CN["p2w"].ap().rearrange("(k p) m -> p k m", p=128), rWPJ2, "dwp2")
            XI3 = XI[:, :].rearrange("p (c x) -> p c x", c=4)
            QKV3 = QKV[:, :].rearrange("p (c x) -> p c x", c=4)

            PA, PB, PC, PS = Fb(0), Fb(1), Fb(2), Fb(3)
            if True:
                rPA, rPB, rPC, rPS = rPF[0], rPF[1], rPF[2], rPF[3]
                pbanks = [(PA, rPA), (PB, rPB)]
                for nt in range(NNT):
                    for m in range(4):
                        ps, rps = pbanks[m % 2]
                        blds = [
                            (lambda m=m, nt=nt, k=k, ps=ps: t_.matmul(
                                ps[:, 0:512],
                                WDOWN[:, 512 * k + 128 * m: 512 * k + 128 * m + 128],
                                XN3[:, k, 512 * nt: 512 * nt + 512],
                                start=(k == 0), stop=(k == 7)))
                            for k in range(8)]
                        P.op("tensor", blds, rd=[rWDOWN, rXN4[nt]], wr=[rps])
                        P.op("scalar", (lambda m=m, nt=nt, ps=ps: s_.activation(
                            XI3[:, m, 512 * nt: 512 * nt + 512], ps[:, 0:512],
                            AF.Identity, bias=CS["dbc"][:, m: m + 1])),
                            rd=[rps, rCS["dbc"]], wr=[rXI])
                for c in range(4):
                    P.op("vector", (lambda c=c: v_.memset(XI3[:, c, 2048:2052], 0.0)),
                         wr=[rXI])
                for nt in range(NNT):
                    for c in range(4):
                        sq = SQ[:, 512 * c: 512 * c + 512]
                        P.op("vector", (lambda c=c, nt=nt, sq=sq: v_.tensor_tensor(
                            sq, XI3[:, c, 512 * nt: 512 * nt + 512],
                            XI3[:, c, 512 * nt: 512 * nt + 512], ALU.mult)),
                            rd=[rXI], wr=[rSQ])
                    for c in range(4):
                        sq = SQ[:, 512 * c: 512 * c + 512]
                        P.op("tensor", (lambda c=c, sq=sq: t_.matmul(
                            PS[0:1, 0:512], CS["onesb"][:, 0:1], sq,
                            start=(c == 0), stop=(c == 3))),
                            rd=[rSQ, rCS["onesb"]], wr=[rPS])
                    P.op("scalar", (lambda: s_.activation(
                        RSTD[0:1, :], PS[0:1, 0:512], AF.Ln,
                        bias=CS["epsc"][0:1, 1:2], scale=1.0 / 512)),
                        rd=[rPS, rCS["epsc"]], wr=[rRSTD])
                    P.op("scalar", (lambda: s_.activation(
                        RSTD[0:1, :], RSTD[0:1, :], AF.Exp, scale=-0.5)),
                        rd=[rRSTD], wr=[rRSTD])
                    P.op("tensor", (lambda: t_.matmul(
                        PC[:, 0:512], CS["onesf"][0:1, 0:128], RSTD[0:1, 0:512],
                        start=True, stop=True)),
                        rd=[rRSTD, rCS["onesf"]], wr=[rPC])
                    for c in range(4):
                        P.op("vector", (lambda c=c, nt=nt: v_.scalar_tensor_tensor(
                            XI3[:, c, 512 * nt: 512 * nt + 512], PC[:, 0:512],
                            CS["n1c"][:, c: c + 1],
                            XI3[:, c, 512 * nt: 512 * nt + 512], ALU.mult, ALU.mult)),
                            rd=[rPC, rCS["n1c"], rXI], wr=[rXI])
                P.replay(blk)

            SCP = PF[:, 0:2048]
            PT = PBF
            PE2 = PF[:, 2048:3072]
            if True:
                rSCB = [rPF[0], rPF[1], rPF[2], rPF[3]]
                rPT = rPBF
                rPEa, rPEb = rPF[4], rPF[5]
                SCP4 = SCP[:, :].rearrange("p (a b x) -> p a b x", a=4, b=4)
                PE2s = PE2[:, :].rearrange("p (b x) -> p b x", b=8)

                def do_tile(br, n, p4, p2, q_ap, k_ap, vdst, rvdst,
                            bias_ap, rd_bias, amask_ap, out_map, rd_qkv,
                            WPJ, rWPJ, PBc, rPBc):
                    nj = n
                    zsl = Z[0:nj, 960 * p4: 960 * p4 + 8 * nj]
                    zv = zsl.rearrange("p (j x) -> p j x", j=8)
                    esl = E[0:nj, 960 * p4: 960 * p4 + 8 * nj]
                    ev = esl.rearrange("p (j x) -> p j x", j=8)

                    def s0():
                        for h in range(8):
                            a, b = h % 4, h // 4
                            P.op("tensor", (lambda h=h, a=a, b=b, nj=nj: t_.matmul(
                                SCP[0:nj, 512 * a + 128 * b: 512 * a + 128 * b + nj],
                                q_ap(h), k_ap(h), start=True, stop=True,
                                tile_position=(32 * a, 0))),
                                rd=rd_qkv, wr=[rSCB[a]])
                        P.op("vector", (lambda nj=nj, bias_ap=bias_ap:
                             v_.tensor_tensor(
                                 zsl, SCP4[0:nj, :, 0:2, 0:nj], bias_ap, ALU.add)),
                            rd=rSCB + rd_bias, wr=[rZ[p4]])
                        if amask_ap is not None:
                            P.op("vector", (lambda: v_.tensor_tensor(
                                zsl, zsl, amask_ap, ALU.add)),
                                rd=[rZ[p4], rCS["am1"]], wr=[rZ[p4]])

                    def s1():
                        P.op("vector", (lambda: v_.reduce_max(
                            NEG[0:nj, 8 * p4: 8 * p4 + 8], zv,
                            axis=AX.X, negate=True)),
                            rd=[rZ[p4]], wr=[rNEGL[p4]])
                        negb = bass.AP(NEG, 8 * p4, [[32, nj], [1, 8], [0, nj]])
                        P.op("vector", (lambda negb=negb: v_.tensor_tensor(
                            zv, zv, negb, ALU.add)),
                            rd=[rZ[p4], rNEGL[p4]], wr=[rZ[p4]])

                    def s2():
                        P.op("scalar", (lambda: s_.activation(esl, zsl, AF.Exp)),
                             rd=[rZ[p4]], wr=[rE[p4]])

                    def s3():
                        P.op("vector", (lambda: v_.reduce_sum(
                            SUM[0:nj, 8 * p4: 8 * p4 + 8], ev, axis=AX.X)),
                            rd=[rE[p4]], wr=[rSUML[p4]])
                        P.op("vector", (lambda: v_.reciprocal(
                            RCP[0:nj, 8 * p4: 8 * p4 + 8],
                            SUM[0:nj, 8 * p4: 8 * p4 + 8])),
                            rd=[rSUML[p4]], wr=[rRCPL[p4]])
                        rcpb = bass.AP(RCP, 8 * p4, [[32, nj], [1, 8], [0, nj]])
                        P.op("vector", (lambda rcpb=rcpb: v_.tensor_tensor(
                            ev, ev, rcpb, ALU.mult)),
                            rd=[rRCPL[p4], rE[p4]], wr=[rE[p4]])

                    def s4():
                        for j in range(8):
                            P.op("tensor", (lambda j=j: t_.transpose(
                                PT[0:nj, 1024 * p2 + 128 * j: 1024 * p2 + 128 * j + nj],
                                E[0:nj, 960 * p4 + nj * j: 960 * p4 + nj * j + nj],
                                CS["idb"][0:nj, 0:nj])),
                                rd=[rE[p4], rCS["idb"]], wr=[rPT[p2]])
                        P.op("scalar", (lambda: s_.copy(
                            PTS[0:nj, 1024 * p2: 1024 * p2 + 8 * nj],
                            PT[0:nj, 1024 * p2: 1024 * p2 + 1024].rearrange(
                                "p (j x) -> p j x", j=8)[:, :, 0:nj])),
                            rd=[rPT[p2]], wr=[rPTS[p2]])

                    def s5():
                        for j in range(8):
                            h = JORDER[j]
                            a, b = h % 4, h // 4
                            P.op("tensor", (lambda j=j, h=h, a=a, b=b: t_.matmul(
                                PE2[32 * a: 32 * a + 32, 128 * b: 128 * b + nj],
                                vdst[0:nj, 32 * h: 32 * h + 32],
                                PTS[0:nj, 1024 * p2 + nj * j: 1024 * p2 + nj * j + nj],
                                start=True, stop=True, tile_position=(0, 32 * a))),
                                rd=[rvdst, rPTS[p2]], wr=[rPEa])
                        P.op("scalar", (lambda: s_.copy(
                            AVS[:, 240 * p2: 240 * p2 + 2 * nj], PE2s[:, 0:2, 0:nj])),
                            rd=[rPEa], wr=[rAVS[p2]])

                    def s6():
                        for mc in range(2):
                            pcol = 256 if mc == 0 else 640
                            rpe = rPEa if mc == 0 else rPEb
                            blds = [
                                (lambda kc=kc, mc=mc, pcol=pcol: t_.matmul(
                                    PE2[0:128, pcol: pcol + nj],
                                    WPJ[:, 256 * kc + 128 * mc:
                                        256 * kc + 128 * mc + 128],
                                    AVS[:, 240 * p2 + nj * kc: 240 * p2 + nj * kc + nj],
                                    start=(kc == 0), stop=(kc == 1)))
                                for kc in range(2)]
                            P.op("tensor", blds, rd=[rWPJ, rAVS[p2]], wr=[rpe])
                            for (plo, wdt, ucol) in out_map:
                                P.op("scalar", (lambda plo=plo, wdt=wdt, ucol=ucol,
                                                mc=mc, br=br, pcol=pcol: s_.activation(
                                    UPIN3[:, 2 * br + mc, ucol: ucol + wdt],
                                    PE2[0:128, pcol + plo: pcol + plo + wdt],
                                    AF.Identity, bias=PBc[:, mc: mc + 1])),
                                    rd=[rpe, rPBc], wr=upin_res(ucol, ucol + wdt))

                    return [s0, s1, s2, (lambda: None), s3, s4, s5, s6]

                for br in range(2):
                    WQ, rWQ = (WQ1, rWQ1) if br == 0 else (WQ2, rWQ2)
                    WPJ, rWPJ = (WPJ1, rWPJ1) if br == 0 else (WPJ2, rWPJ2)
                    PBc, rPBc = ((CS["pb1"], rCS["pb1"]) if br == 0
                                 else (CS["pb2"], rCS["pb2"]))
                    vb, rvb = ((CS["vb1"], rCS["vb1"]) if br == 0
                               else (CS["vb2"], rCS["vb2"]))
                    qb, rqb = ((CS["qb1"], rCS["qb1"]) if br == 0
                               else (CS["qb2"], rCS["qb2"]))
                    btn = "bt1" if br == 0 else "bt2"
                    off = 5 if br == 0 else 10
                    ntiles_n = [(nt, 512 * nt, 512) for nt in range(NNT)]
                    if br == 1:
                        ntiles_n.append((0, 2048, 4))
                    for m in range(4):
                        for (bidx, col0, wdt) in ntiles_n:
                            bank = (bidx if wdt == 512 else 3)
                            ps_ap = SCP[:, 512 * bank: 512 * bank + wdt]
                            rps = rSCB[bank]
                            blds = [
                                (lambda m=m, k=k, col0=col0, wdt=wdt, ps_ap=ps_ap,
                                 br=br, WQ=WQ: t_.matmul(
                                     ps_ap,
                                     WQ[:, 768 * k + 128 * m: 768 * k + 128 * m + 128],
                                     XI3[:, 2 * br + k, col0: col0 + wdt],
                                     start=(k == 0), stop=(k == 1)))
                                for k in range(2)]
                            P.op("tensor", blds, rd=[rWQ, rXI], wr=[rps])
                            bias = (QBS[:, 2 * br + m: 2 * br + m + 1] if m < 2
                                    else qb[:, m: m + 1])
                            scl = QSCALE if m < 2 else 1.0
                            rb = rQBS if m < 2 else rqb
                            P.op("scalar", (lambda m=m, col0=col0, wdt=wdt,
                                            ps_ap=ps_ap, bias=bias, scl=scl:
                                 s_.activation(
                                     QKV3[:, m, col0: col0 + wdt], ps_ap,
                                     AF.Identity, bias=bias, scale=scl)),
                                rd=[rps, rb], wr=[rQKV])
                    for t in range(17):
                        tok0 = off + 120 * t
                        pvs = PE2[0:120, 0:256] if t % 2 == 0 else PE2[0:120, 512:768]
                        rpv = rPEa if t % 2 == 0 else rPEb
                        blds = [
                            (lambda k=k, tok0=tok0, br=br, WQ=WQ, pvs=pvs: t_.matmul(
                                pvs, XI3[:, 2 * br + k, tok0: tok0 + 120],
                                WQ[:, 768 * k + 512: 768 * k + 768],
                                start=(k == 0), stop=(k == 1)))
                            for k in range(2)]
                        P.op("tensor", blds, rd=[rXI, rWQ], wr=[rpv])
                        P.op("vector", (lambda t=t, pvs=pvs, vb=vb: v_.tensor_tensor(
                            VT17[0:120, 256 * t: 256 * t + 256], pvs,
                            vb[0:120, 0:256], ALU.add)),
                            rd=[rpv, rvb], wr=[rVT17])
                    stages = []
                    for t in range(17):
                        tok0 = off + 120 * t
                        wdt = 120 if (br == 0 or t < 16) else 118
                        q_ap = (lambda h, tok0=tok0: QKV3[
                            32 * (h % 4): 32 * (h % 4) + 32, h // 4, tok0: tok0 + 120])
                        k_ap = (lambda h, tok0=tok0: QKV3[
                            32 * (h % 4): 32 * (h % 4) + 32, 2 + h // 4,
                            tok0: tok0 + 120])
                        stages.append(do_tile(br, 120, t % 4, t % 2, q_ap, k_ap,
                                      VT17[:, 256 * t: 256 * t + 256], rVT17,
                                      CS[btn][0:120, 0:960], [rCS[btn]], None,
                                      [(0, wdt, tok0)], [rQKV],
                                      WPJ, rWPJ, PBc, rPBc))
                    NSTG = 8
                    for step in range(17 + NSTG - 1):
                        for sidx in range(NSTG):
                            t = step - sidx
                            if 0 <= t < 17:
                                stages[t][sidx]()
                    if br == 0:
                        for c in range(2):
                            P.op("vector", (lambda c=c: v_.tensor_copy(
                                XS[:, 16 * c: 16 * c + 3], XI3[:, c, 2045: 2048])),
                                rd=[rXI], wr=[rXS])
                            P.op("vector", (lambda c=c: v_.memset(
                                XS[:, 16 * c + 3: 16 * c + 5], 0.0)), wr=[rXS])
                            P.op("vector", (lambda c=c: v_.tensor_copy(
                                XS[:, 16 * c + 5: 16 * c + 10], XI3[:, c, 0: 5])),
                                rd=[rXI], wr=[rXS])
                        sq_src = (lambda k: XS[:, 16 * k: 16 * k + 10])
                        rd_src = [rXS]
                        out_map = [(0, 3, 2045), (5, 5, 0)]
                        bias_ap = CS["bs1"][0:10, 0:80]
                        rd_bias = [rCS["bs1"]]
                        amask_ap = CS["am1"][0:10, 0:80]
                    else:
                        sq_src = (lambda k: XI3[:, 2 + k, 0: 10])
                        rd_src = [rXI]
                        out_map = [(0, 10, 0)]
                        bias_ap = CS["bs2"][0:10, 0:80]
                        rd_bias = [rCS["bs2"]]
                        amask_ap = None
                    for m in range(4):
                        blds = [
                            (lambda m=m, k=k, WQ=WQ, sq_src=sq_src: t_.matmul(
                                SCP[0:128, 1536 + 16 * m: 1536 + 16 * m + 10],
                                WQ[:, 768 * k + 128 * m: 768 * k + 128 * m + 128],
                                sq_src(k), start=(k == 0), stop=(k == 1)))
                            for k in range(2)]
                        P.op("tensor", blds, rd=rd_src + [rWQ], wr=[rSCB[3]])
                        bias = (QBS[:, 2 * br + m: 2 * br + m + 1] if m < 2
                                else qb[:, m: m + 1])
                        scl = QSCALE if m < 2 else 1.0
                        rb = rQBS if m < 2 else rqb
                        P.op("scalar", (lambda m=m, bias=bias, scl=scl: s_.activation(
                            QKVS[:, 16 * m: 16 * m + 10],
                            SCP[0:128, 1536 + 16 * m: 1536 + 16 * m + 10],
                            AF.Identity, bias=bias, scale=scl)),
                            rd=[rSCB[3], rb], wr=[rQKVS])
                    q_ap = (lambda h: QKVS[32 * (h % 4): 32 * (h % 4) + 32,
                                           16 * (h // 4): 16 * (h // 4) + 10])
                    k_ap = (lambda h: QKVS[32 * (h % 4): 32 * (h % 4) + 32,
                                           16 * (2 + h // 4): 16 * (2 + h // 4) + 10])
                    v_blds = [
                        (lambda k=k, WQ=WQ, sq_src=sq_src: t_.matmul(
                            PE2[0:10, 0:256], sq_src(k),
                            WQ[:, 768 * k + 512: 768 * k + 768],
                            start=(k == 0), stop=(k == 1)))
                        for k in range(2)]
                    P.op("tensor", v_blds, rd=rd_src + [rWQ], wr=[rPEa])
                    P.op("vector", (lambda vb=vb: v_.tensor_tensor(
                        VS[0:10, :], PE2[0:10, 0:256], vb[0:10, 0:256], ALU.add)),
                        rd=[rPEa, rvb], wr=[rVS])
                    for st in do_tile(br, 10, 0, 0, q_ap, k_ap,
                                      VS, rVS, bias_ap, rd_bias, amask_ap,
                                      out_map, [rQKVS],
                                      WPJ, rWPJ, PBc, rPBc):
                        st()
                P.replay(blk)
        esA.close()
        P.fence()

        esA4 = ExitStack()
        H = esA4.enter_context(nc.sbuf_tensor("H", [128, 8 * 1024], BF16))
        MID = esA4.enter_context(nc.sbuf_tensor("MID", [128, 16 * 512], BF16))
        RST2 = esA4.enter_context(nc.sbuf_tensor("RST2", [1, 2 * 512], F32))
        WF1 = esA4.enter_context(nc.sbuf_tensor("WF1", [128, 8 * 2048], BF16))
        WF2 = esA4.enter_context(nc.sbuf_tensor("WF2", [128, 16 * 1024], BF16))
        PA, PB, PC = Fb(0), Fb(1), Fb(2)
        PS = PF[:, 1024:1536]
        PT = PBF
        PE2 = PF[:, 2048:3072]
        PE2H = Fb(3)
        if True:
            rH = [Res("H0"), Res("H1")]
            rMID = Res("MID")
            rRST2 = Res("RST2")
            rWF1, rWF2 = Res("WF1"), Res("WF2")
            rPA, rPB, rPC, rPS = rPF[0], rPF[1], rPF[2], rPF[2]
            rPT = rPBF
            rPE2 = rPF[4]
            rPE2F5 = rPF[5]
            rPE2H = rPF[3]
            wload(WF1[:, :].rearrange("p (k m) -> p k m", k=8),
                  CN["f1w"].ap().rearrange("(k p) m -> p k m", p=128), rWF1, "dwf1")
            wload(WF2[:, :].rearrange("p (k m) -> p k m", k=16),
                  CN["f2w"].ap().rearrange("(k p) m -> p k m", p=128), rWF2, "dwf2")
            MID3 = MID[:, :].rearrange("p (c x) -> p c x", c=16)
            XA3L = [MID3[:, 8:16, :], MID3[:, 0:8, :]]
            rXAL = [rMID, rMID]
            H3 = H[:, :].rearrange("p (c x) -> p c x", c=8)
            pbanks = [(PA, rPA), (PB, rPB)]

            def a4_up(nt):
                hs = nt % 2
                XA3 = XA3L[hs]
                for m in range(8):
                    ps, rps = pbanks[m % 2]
                    blds = [
                        (lambda m=m, nt=nt, k=k, ps=ps: t_.matmul(
                            ps[:, 0:512],
                            WUP[:, 1024 * k + 128 * m: 1024 * k + 128 * m + 128],
                            UPIN3[:, k, 512 * nt: 512 * nt + 512],
                            start=(k == 0), stop=(k == 3)))
                        for k in range(4)]
                    P.op("tensor", blds,
                         rd=[rWUP] + upin_res(512 * nt, 512 * nt + 512), wr=[rps])
                    P.op("vector", (lambda m=m, nt=nt, ps=ps, XA3=XA3:
                         v_.scalar_tensor_tensor(
                             XA3[:, m, 0:512], ps[:, 0:512], CS["ubc"][:, m: m + 1],
                             XN3[:, m, 512 * nt: 512 * nt + 512], ALU.add, ALU.add)),
                        rd=[rps, rCS["ubc"], rXN4[nt]], wr=[rXAL[hs]])

            def a4_trans(nt):
                hs = nt % 2
                XA3 = XA3L[hs]
                for tcl in range(4):
                    tc = 4 * nt + tcl
                    pp = tcl % 2
                    for c in range(8):
                        P.op("tensor", (lambda c=c, tcl=tcl, pp=pp, XA3=XA3:
                             t_.transpose(
                                 PT[:, 1024 * pp + 128 * c: 1024 * pp + 128 * c + 128],
                                 XA3[:, c, 128 * tcl: 128 * tcl + 128],
                                 CS["idb"][:, :])),
                            rd=[rXAL[hs], rCS["idb"]], wr=[rPT[pp]])
                    P.op("vector", (lambda tc=tc, pp=pp: v_.tensor_tensor(
                        S[:, 1024 * tc: 1024 * tc + 1024],
                        PT[:, 1024 * pp: 1024 * pp + 1024],
                        S[:, 1024 * tc: 1024 * tc + 1024], ALU.add)),
                        rd=[rPT[pp]], wr=[rS[tc]])

            def a4_sq(nt):
                hs = nt % 2
                XA3 = XA3L[hs]
                for c in range(8):
                    sq = H3[:, c, 512 * (1 - hs): 512 * (1 - hs) + 512]
                    P.op("vector", (lambda c=c, sq=sq, XA3=XA3: v_.tensor_tensor(
                        sq, XA3[:, c, 0:512], XA3[:, c, 0:512], ALU.mult)),
                        rd=[rXAL[hs]], wr=[rH[1 - hs]])

            def a4_stats(nt):
                hs = nt % 2
                for c in range(8):
                    sq = H3[:, c, 512 * (1 - hs): 512 * (1 - hs) + 512]
                    P.op("tensor", (lambda c=c, sq=sq: t_.matmul(
                        PS[0:1, 0:512], CS["onesb"][:, 0:1], sq,
                        start=(c == 0), stop=(c == 7))),
                        rd=[rH[1 - hs], rCS["onesb"]], wr=[rPS])
                P.op("scalar", (lambda hs=hs: s_.activation(
                    RST2[0:1, 512 * hs: 512 * hs + 512], PS[0:1, 0:512], AF.Ln,
                    bias=CS["epsc"][0:1, 1:2], scale=1.0 / 1024)),
                    rd=[rPS, rCS["epsc"]], wr=[rRST2])
                P.op("scalar", (lambda hs=hs: s_.activation(
                    RST2[0:1, 512 * hs: 512 * hs + 512],
                    RST2[0:1, 512 * hs: 512 * hs + 512], AF.Exp, scale=-0.5)),
                    rd=[rRST2], wr=[rRST2])

            def a4_h(nt):
                hs = nt % 2
                XA3 = XA3L[hs]
                P.op("tensor", (lambda hs=hs: t_.matmul(
                    PC[:, 0:512], CS["onesf"][0:1, 0:128],
                    RST2[0:1, 512 * hs: 512 * hs + 512],
                    start=True, stop=True)),
                    rd=[rRST2, rCS["onesf"]], wr=[rPC])
                for c in range(8):
                    P.op("vector", (lambda c=c, hs=hs, XA3=XA3:
                         v_.scalar_tensor_tensor(
                             H3[:, c, 512 * hs: 512 * hs + 512], PC[:, 0:512],
                             CS["n2c"][:, c: c + 1],
                             XA3[:, c, 0:512], ALU.mult, ALU.mult)),
                        rd=[rPC, rCS["n2c"], rXAL[hs]], wr=[rH[hs]])

            def a4_ffn1(nt):
                hs = nt % 2
                for m in range(16):
                    ps, rps = pbanks[m % 2]
                    blds = [
                        (lambda m=m, k=k, ps=ps, hs=hs: t_.matmul(
                            ps[:, 0:512],
                            WF1[:, 2048 * k + 128 * m: 2048 * k + 128 * m + 128],
                            H3[:, k, 512 * hs: 512 * hs + 512],
                            start=(k == 0), stop=(k == 7)))
                        for k in range(8)]
                    P.op("tensor", blds, rd=[rWF1, rH[hs]], wr=[rps])
                    P.op("scalar", (lambda m=m, ps=ps: s_.activation(
                        MID3[:, m, 0:512], ps[:, 0:512], AF.Gelu,
                        bias=CS["fb1c"][:, m: m + 1])),
                        rd=[rps, rCS["fb1c"]], wr=[rMID])

            def a4_ffn2(nt):
                for tcl in range(4):
                    tc = 4 * nt + tcl
                    if tcl % 2 == 0:
                        halves = [(PE2[0:128, 0:512], rPE2),
                                  (PE2[0:128, 512:1024], rPE2F5)]
                    else:
                        halves = [(PE2H[0:128, 0:512], rPE2H),
                                  (PC[0:128, 0:512], rPC)]
                    blds = []
                    for h2 in range(2):
                        ph = halves[h2][0]
                        for k in range(16):
                            blds.append(lambda k=k, h2=h2, tcl=tcl, ph=ph: t_.matmul(
                                ph, MID3[:, k, 128 * tcl: 128 * tcl + 128],
                                WF2[:, 1024 * k + 512 * h2: 1024 * k + 512 * h2 + 512],
                                start=(k == 0), stop=False))
                        blds.append(lambda h2=h2, ph=ph: t_.matmul(
                            ph, CS["onesb"][0:1, 0:128],
                            FB2R[0:1, 512 * h2: 512 * h2 + 512],
                            start=False, stop=True))
                    P.op("tensor", blds,
                         rd=[rMID, rWF2, rFB2R, rCS["onesb"]],
                         wr=[halves[0][1], halves[1][1]])
                    if tcl % 2 == 0:
                        P.op("vector", (lambda tc=tc: v_.tensor_tensor(
                            S[:, 1024 * tc: 1024 * tc + 1024], PE2[0:128, 0:1024],
                            S[:, 1024 * tc: 1024 * tc + 1024], ALU.add)),
                            rd=[rPE2, rPE2F5], wr=[rS[tc]])
                    else:
                        for h2 in range(2):
                            ph, rph = halves[h2]
                            P.op("vector", (lambda tc=tc, h2=h2, ph=ph:
                                 v_.tensor_tensor(
                                     S[:, 1024 * tc + 512 * h2:
                                       1024 * tc + 512 * h2 + 512],
                                     ph,
                                     S[:, 1024 * tc + 512 * h2:
                                       1024 * tc + 512 * h2 + 512],
                                     ALU.add)),
                                rd=[rph], wr=[rS[tc]])

            for pr in range(2):
                n0, n1 = 2 * pr, 2 * pr + 1
                a4_up(n0)
                a4_up(n1)
                a4_sq(n0)
                a4_trans(n0)
                a4_stats(n0)
                a4_sq(n1)
                a4_trans(n1)
                a4_stats(n1)
                a4_h(n0)
                a4_h(n1)
                a4_ffn1(n0)
                a4_ffn2(n0)
                a4_ffn1(n1)
                a4_ffn2(n1)
            P.fence()
            P.replay(blk)
        esA4.close()
    esU.close()

    esM = ExitStack()
    OST = esM.enter_context(nc.sbuf_tensor("OST", [128, 2 * 1024], F32))
    EB2R = esM.enter_context(nc.sbuf_tensor("EB2R", [8, 1024], BF16))
    rOST = [Res("OST0"), Res("OST1")]
    WE1 = esM.enter_context(nc.sbuf_tensor("WE1", [128, 2 * 8 * 2048], BF16))
    WE2 = esM.enter_context(nc.sbuf_tensor("WE2", [128, 16 * 1024], BF16))
    EHS = esM.enter_context(nc.sbuf_tensor("EHS", [128, 16 * 512], BF16))
    PA, PB = Fb(0), Fb(1)
    PE2A = PF[:, 1024:2048]
    PE2B = PF[:, 2048:3072]
    if True:
        rWE1Q = [[Res(f"we1_{sl}_{q}") for q in range(4)] for sl in range(2)]
        rWE2Q = [Res(f"we2_{q}") for q in range(4)]
        rEHSlo, rEHShi = Res("EHSlo"), Res("EHShi")
        rPA, rPB = rPF[0], rPF[1]
        rPE2L = [[rPF[2], rPF[3]], [rPF[4], rPF[5]]]
        PE2L = [PE2A, PE2B]
        WE13 = [WE1[:, 0: 8 * 2048].rearrange("p (k m) -> p k m", k=8),
                WE1[:, 8 * 2048: 16 * 2048].rearrange("p (k m) -> p k m", k=8)]
        WE23 = WE2[:, :].rearrange("p (k m) -> p k m", k=16)
        EHS3 = EHS[:, :].rearrange("p (c x) -> p c x", c=16)
        e1hbm = CN["e1w"].ap().rearrange("e (k p) m -> e p k m", p=128)
        e2hbm = CN["e2w"].ap().rearrange("e (k p) m -> e p k m", p=128)
        pbanks = [(PA, rPA), (PB, rPB)]
        P.op("gpsimd", (lambda: g_.dma_start(EB2R[:, :], CN["eb2"].ap())),
             wr=[rEB2R], dma="dcg2")
        for e in range(8):
            slot = e % 2
            for q in range(4):
                P.op("gpsimd", (lambda e=e, q=q, slot=slot: g_.dma_start(
                    WE13[slot][:, :, 512 * q: 512 * q + 512],
                    e1hbm[e, :, :, 512 * q: 512 * q + 512])),
                    wr=[rWE1Q[slot][q]], dma=f"de1{slot}{q}")
            for q in range(4):
                P.op("gpsimd", (lambda e=e, q=q: g_.dma_start(
                    WE23[:, 4 * q: 4 * q + 4, :],
                    e2hbm[e, :, 4 * q: 4 * q + 4, :])),
                    wr=[rWE2Q[q]], dma=f"de2{q}")
            for nt in range(NNT):
                for m in range(16):
                    ps, rps = pbanks[m % 2]
                    blds = [
                        (lambda m=m, nt=nt, k=k, ps=ps, slot=slot: t_.matmul(
                            ps[:, 0:512],
                            WE13[slot][:, k, 128 * m: 128 * m + 128],
                            XN3[:, k, 512 * nt: 512 * nt + 512],
                            start=(k == 0), stop=(k == 7)))
                        for k in range(8)]
                    P.op("tensor", blds, rd=[rWE1Q[slot][m // 4], rXN4[nt]], wr=[rps])
                    P.op("scalar", (lambda e=e, m=m, ps=ps: s_.activation(
                        EHS3[:, m, 0:512], ps[:, 0:512], AF.Silu,
                        bias=CS["eb1c"][:, 16 * e + m: 16 * e + m + 1])),
                        rd=[rps, rCS["eb1c"]], wr=[rEHSlo if m < 8 else rEHShi])
                for tcl in range(4):
                    tc = 4 * nt + tcl
                    PEx = PE2L[tcl % 2]
                    rPEx = rPE2L[tcl % 2]
                    blds_lo, blds_hi = [], []
                    for h2 in range(2):
                        for k in range(16):
                            (blds_lo if k < 8 else blds_hi).append(
                                lambda k=k, h2=h2, tcl=tcl, e=e, PEx=PEx: t_.matmul(
                                    PEx[0:128, 512 * h2: 512 * h2 + 512],
                                    EHS3[:, k, 128 * tcl: 128 * tcl + 128],
                                    WE23[:, k, 512 * h2: 512 * h2 + 512],
                                    start=(k == 0),
                                    stop=(False if e == 7 else k == 15)))
                        if e == 7:
                            blds_hi.append(lambda h2=h2, tc=tc, PEx=PEx: t_.matmul(
                                PEx[0:128, 512 * h2: 512 * h2 + 512],
                                WFM[0:8, 128 * tc: 128 * tc + 128],
                                EB2R[0:8, 512 * h2: 512 * h2 + 512],
                                start=False, stop=True))
                    rdl_hi = [rEHShi, rWE2Q[2], rWE2Q[3]]
                    if e == 7:
                        rdl_hi += [rWFM, rEB2R]
                    P.op("tensor", blds_lo,
                         rd=[rEHSlo, rWE2Q[0], rWE2Q[1]], wr=rPEx)
                    P.op("tensor", blds_hi, rd=rdl_hi, wr=rPEx)
                    P.op("vector", (lambda e=e, tc=tc, PEx=PEx: v_.scalar_tensor_tensor(
                        S[:, 1024 * tc: 1024 * tc + 1024], PEx[0:128, 0:1024],
                        WTM[:, 8 * tc + e: 8 * tc + e + 1],
                        S[:, 1024 * tc: 1024 * tc + 1024], ALU.mult, ALU.add)),
                        rd=rPEx + [rWTM], wr=[rS[tc]])
                    if e == 7:
                        par = tc % 2
                        P.op("scalar", (lambda tc=tc, par=par: s_.copy(
                            OST[:, 1024 * par: 1024 * par + 1024],
                            S[:, 1024 * tc: 1024 * tc + 1024])),
                            rd=[rS[tc]], wr=[rOST[par]])
                        P.op("sync", (lambda tc=tc, par=par: y_.dma_start(
                            outd.ap()[128 * tc: 128 * tc + 128, :],
                            OST[:, 1024 * par: 1024 * par + 1024])),
                            rd=[rOST[par]], dma=f"do{par}")
        P.fence()
        P.replay(blk)
    esM.close()

    if True:
        P.fence_pend["sync"]["do0"] = P.cnt.get("do0", 0)
        P.fence_pend["sync"]["do1"] = P.cnt.get("do1", 0)
        P.op("sync", (lambda: y_.nop()))
        P.replay(blk)

    blk_cm.__exit__(None, None, None)
    es.close()
    return nc


_NC_CACHE = None


def kernel(**inp):
    global _NC_CACHE
    x = np.ascontiguousarray(np.asarray(inp["L0_attn_x"], np.float32))
    consts = build_consts(inp)
    if _NC_CACHE is None:
        _NC_CACHE = build_nc()
    nc = _NC_CACHE
    in_maps = []
    for c in range(8):
        m = {"x": np.ascontiguousarray(x[c])}
        m.update(consts)
        in_maps.append(m)
    res = run_bass_kernel_spmd(nc, in_maps, core_ids=list(range(8)))
    return np.stack([np.asarray(r["out"], np.float32) for r in res.results], 0)


# version 26
# speedup vs baseline: 1.0468x; 1.0468x over previous
import numpy as np
import ml_dtypes
from contextlib import ExitStack

import concourse.bass as bass
import concourse.mybir as mybir
from concourse.bass_utils import run_bass_kernel_spmd

F32 = mybir.dt.float32
BF16 = mybir.dt.bfloat16
AF = mybir.ActivationFunctionType
ALU = mybir.AluOpType
AX = mybir.AxisListType
BFNP = ml_dtypes.bfloat16

T = 2048
D = 1024
NTC = 16
NNT = 4
QSCALE = 0.17677669529663687
JORDER = [0, 4, 1, 5, 2, 6, 3, 7]

ENGSEM = {"tensor": "t", "vector": "v", "scalar": "s", "gpsimd": "g", "sync": "y"}


class Res:
    __slots__ = ("w", "rs", "name")
    def __init__(self, name=""):
        self.w = []
        self.rs = []
        self.name = name


class Prog:
    def __init__(self, nc, es):
        self.nc = nc
        self.q = {e: [] for e in ENGSEM}
        self.cnt = {}
        self.waited = {e: {} for e in ENGSEM}
        self.fence_pend = {e: {} for e in ENGSEM}
        self.sems = {}
        self.es = es

    def sem(self, name):
        if name not in self.sems:
            self.sems[name] = self.es.enter_context(self.nc.semaphore(name))
            self.cnt[name] = 0
        return self.sems[name]

    def op(self, engine, builders, rd=(), wr=(), dma=None):
        if callable(builders):
            builders = [builders]
        deps = {}
        def add(tok):
            s, v = tok
            if deps.get(s, 0) < v:
                deps[s] = v
        for r in rd:
            for tk in r.w:
                add(tk)
        for w in wr:
            for tk in w.w:
                add(tk)
            for tk in w.rs:
                add(tk)
        for s, v in self.fence_pend[engine].items():
            if deps.get(s, 0) < v:
                deps[s] = v
        self.fence_pend[engine] = {}
        waits = []
        wt = self.waited[engine]
        for s, v in sorted(deps.items()):
            if v > wt.get(s, 0):
                waits.append((s, v))
                wt[s] = v
        if dma is not None:
            semname, amt = dma, 16
        else:
            semname, amt = ENGSEM[engine], 1
        self.sem(semname)
        self.cnt[semname] += amt
        tok = (semname, self.cnt[semname])
        self.q[engine].append((waits, builders, semname, amt))
        for r in rd:
            r.rs.append(tok)
        for w in wr:
            if w.rs:
                w.w = [tok]
                w.rs = []
            else:
                w.w.append(tok)
        return tok

    def fence(self, engines=None):
        snap = dict(self.cnt)
        for e in (engines or ENGSEM):
            fp = self.fence_pend[e]
            for s, v in snap.items():
                if fp.get(s, 0) < v:
                    fp[s] = v

    def replay(self, block):
        handles = {"tensor": block.tensor, "vector": block.vector,
                   "scalar": block.scalar, "gpsimd": block.gpsimd,
                   "sync": block.sync}
        for ename, deco in handles.items():
            queue = self.q[ename]
            if not queue:
                continue
            sems = self.sems
            def body(eng, queue=list(queue), sems=sems):
                for waits, builders, semname, amt in queue:
                    for s, v in waits:
                        eng.wait_ge(sems[s], v)
                    for b in builders[:-1]:
                        b()
                    builders[-1]().then_inc(sems[semname], amt)
            deco(body)
            self.q[ename] = []


def _pack_vec(v, chunks):
    return np.ascontiguousarray(np.asarray(v, np.float32).reshape(chunks, 128).T)


def _bias_tile(rpb, w, G):
    n = G * w
    coords = np.arange(w)
    ridx = coords[:, None] - coords[None, :] + w - 1
    out = np.full((8, n, n), -30000.0, np.float32)
    for g in range(G):
        sl = slice(g * w, (g + 1) * w)
        for h in range(8):
            out[h][sl, sl] = np.asarray(rpb, np.float32)[ridx, h]
    return np.ascontiguousarray(np.concatenate([out[h] for h in JORDER], axis=1))


def _special_tile(rpb, w, idx):
    idx = np.asarray(idx)
    ridx = idx[:, None] - idx[None, :] + w - 1
    out = np.stack([np.asarray(rpb, np.float32)[ridx, h] for h in range(8)], 0)
    return np.ascontiguousarray(np.concatenate([out[h] for h in JORDER], axis=1))


def build_consts(inp):
    c = {}
    c["idb"] = np.eye(128, dtype=BFNP)
    c["epsc"] = np.ascontiguousarray(
        np.broadcast_to(np.array([1e-5, 1e-6], np.float32), (128, 2)))
    c["idf"] = np.eye(128, dtype=np.float32)
    c["onesb"] = np.ones((128, 128), dtype=BFNP)
    c["onesf"] = np.ones((1, 128), dtype=np.float32)
    c["bt1"] = _bias_tile(inp["L0_attn_rpb1"], 10, 12)
    c["bt2"] = _bias_tile(inp["L0_attn_rpb2"], 20, 6)
    c["bs1"] = _special_tile(inp["L0_attn_rpb1"], 10, np.arange(10))
    c["bs2"] = _special_tile(inp["L0_attn_rpb2"], 20, np.arange(10, 20))
    lab = np.array([1, 1, 1, 1, 1, 2, 2, 2, 2, 2], np.float32)
    am = np.where(lab[:, None] != lab[None, :], -100.0, 0.0).astype(np.float32)
    c["am1"] = np.ascontiguousarray(np.tile(am, (1, 8)))
    c["vb1"] = np.ascontiguousarray(
        np.broadcast_to(np.asarray(inp["L0_attn_qkv1_b"], np.float32)[512:768], (128, 256)))
    c["vb2"] = np.ascontiguousarray(
        np.broadcast_to(np.asarray(inp["L0_attn_qkv2_b"], np.float32)[512:768], (128, 256)))
    c["dbc"] = _pack_vec(inp["L0_attn_down_b"], 4)
    c["n1c"] = _pack_vec(inp["L0_attn_norm1_w"], 4)
    c["n2c"] = _pack_vec(inp["L0_attn_norm2_w"], 8)
    c["ubc"] = _pack_vec(inp["L0_attn_up_b"], 8)
    c["qb1"] = _pack_vec(np.asarray(inp["L0_attn_qkv1_b"], np.float32)[:512], 4)
    c["qb2"] = _pack_vec(np.asarray(inp["L0_attn_qkv2_b"], np.float32)[:512], 4)
    c["pb1"] = _pack_vec(inp["L0_attn_proj1_b"], 2)
    c["pb2"] = _pack_vec(inp["L0_attn_proj2_b"], 2)
    c["fb1c"] = _pack_vec(inp["L0_attn_ffn_b1"], 16)
    c["rb1c"] = _pack_vec(inp["L0_ffn_rb1"], 8)
    c["rb2c"] = np.ascontiguousarray(np.asarray(inp["L0_ffn_rb2"], np.float32).reshape(8, 1))
    eb1 = np.asarray(inp["L0_ffn_eb1"], np.float32)
    c["eb1c"] = np.ascontiguousarray(
        np.concatenate([_pack_vec(eb1[e], 16) for e in range(8)], axis=1))
    c["fb2"] = np.ascontiguousarray(np.asarray(inp["L0_attn_ffn_b2"], np.float32).reshape(1, 1024))
    c["eb2"] = np.ascontiguousarray(np.asarray(inp["L0_ffn_eb2"], np.float32))
    for n, k in [("dw", "L0_attn_down_w"), ("q1w", "L0_attn_qkv1_w"),
                 ("q2w", "L0_attn_qkv2_w"), ("p1w", "L0_attn_proj1_w"),
                 ("p2w", "L0_attn_proj2_w"), ("uw", "L0_attn_up_w"),
                 ("f1w", "L0_attn_ffn_w1"), ("f2w", "L0_attn_ffn_w2"),
                 ("r1w", "L0_ffn_rw1"), ("r2w", "L0_ffn_rw2"),
                 ("e1w", "L0_ffn_ew1"), ("e2w", "L0_ffn_ew2")]:
        c[n] = np.ascontiguousarray(np.asarray(inp[k], np.float32))
    return c


CSHAPES = {
    "idb": ([128, 128], BF16), "idf": ([128, 128], F32),
    "epsc": ([128, 2], F32),
    "onesb": ([128, 128], BF16), "onesf": ([1, 128], F32),
    "bt1": ([120, 960], F32), "bt2": ([120, 960], F32),
    "bs1": ([10, 80], F32), "bs2": ([10, 80], F32), "am1": ([10, 80], F32),
    "vb1": ([128, 256], F32), "vb2": ([128, 256], F32),
    "dbc": ([128, 4], F32), "n1c": ([128, 4], F32), "n2c": ([128, 8], F32),
    "ubc": ([128, 8], F32), "qb1": ([128, 4], F32), "qb2": ([128, 4], F32),
    "pb1": ([128, 2], F32), "pb2": ([128, 2], F32), "fb1c": ([128, 16], F32),
    "rb1c": ([128, 8], F32), "rb2c": ([8, 1], F32), "eb1c": ([128, 128], F32),
}
WSHAPES = {
    "fb2": ([1, 1024], F32), "eb2": ([8, 1024], F32),
    "dw": ([1024, 512], F32), "q1w": ([256, 768], F32), "q2w": ([256, 768], F32),
    "p1w": ([256, 256], F32), "p2w": ([256, 256], F32), "uw": ([512, 1024], F32),
    "f1w": ([1024, 2048], F32), "f2w": ([2048, 1024], F32),
    "r1w": ([1024, 1024], F32), "r2w": ([1024, 8], F32),
    "e1w": ([8, 1024, 2048], F32), "e2w": ([8, 2048, 1024], F32),
}


def build_nc():
    nc = bass.Bass("TRN2", target_bir_lowering=False)
    es = ExitStack()
    P = Prog(nc, es)
    t_, v_, s_, g_, y_ = nc.tensor, nc.vector, nc.scalar, nc.gpsimd, nc.sync

    xd = nc.dram_tensor("x", [T, D], F32, kind="ExternalInput")
    outd = nc.dram_tensor("out", [T, D], F32, kind="ExternalOutput")
    CN = {}
    for n, (shp, dt) in {**CSHAPES, **WSHAPES}.items():
        CN[n] = nc.dram_tensor(n, shp, dt, kind="ExternalInput")

    def sb(name, shape, dt):
        return es.enter_context(nc.sbuf_tensor(name, shape, dt))

    S = sb("S", [128, 16 * 1024], BF16)
    XN = sb("XN", [128, 8 * 2048], BF16)
    WTM = sb("WTM", [128, 16 * 8], F32)
    LTM = sb("LTM", [128, 16 * 8], F32)
    WFM = sb("WFM", [8, 2048], BF16)

    MVS = sb("MVS", [128, 16], F32)

    QBS = sb("QBS", [128, 4], F32)
    BRANCH_CONSTS = ("bt1", "bt2", "bs1", "bs2", "am1", "vb1", "vb2")
    CS = {n: sb("c_" + n, shp, dt) for n, (shp, dt) in CSHAPES.items()
          if n not in BRANCH_CONSTS}
    PF = es.enter_context(nc.psum_tensor("PF", [128, 3072], F32))
    PBF = es.enter_context(nc.psum_tensor("PBF", [128, 2048], BF16))
    rPF = [Res(f"pf{i}") for i in range(6)]
    rPBF = [Res("pbf0"), Res("pbf1")]
    def Fb(i):
        return PF[:, 512 * i: 512 * i + 512]
    FB2R = sb("FB2R", [1, 1024], BF16)

    rS = [Res(f"S{i}") for i in range(NTC)]
    rXN4 = [Res(f"XN{i}") for i in range(4)]
    rWTM, rLTM, rWFM = Res("WTM"), Res("LTM"), Res("WFM")
    rGAT, rMVS, rRSTD, rQBS = Res("GAT"), Res("MVS"), Res("RSTD"), Res("QBS")
    rCS = {n: Res(n) for n in CS}
    rFB2R, rEB2R = Res("FB2R"), Res("EB2R")

    XN3 = XN[:, :].rearrange("p (c x) -> p c x", c=8)

    def wload(dst_ap, src_ap, res, semname):
        P.op("gpsimd", (lambda d=dst_ap, a=src_ap: g_.dma_start(d, a)),
             wr=[res], dma=semname)

    WDOWN = sb("WDOWN", [128, 8 * 512], BF16)
    rWDOWN = Res("WD")
    wload(WDOWN[:, :].rearrange("p (k m) -> p k m", k=8),
          CN["dw"].ap().rearrange("(k p) m -> p k m", p=128), rWDOWN, "dwdn")
    rWR1g, rWR2g = Res("WR1"), Res("WR2")

    blk_cm = nc.Block()
    blk = blk_cm.__enter__()

    def emit_const_loads():
        for n in CS:
            P.op("sync", (lambda d=CS[n], a=CN[n].ap(): y_.dma_start(d[:, :], a)),
                 wr=[rCS[n]], dma="dc_" + n)
    for n in BRANCH_CONSTS:
        rCS[n] = Res(n)
    P.op("gpsimd", (lambda: g_.dma_start(FB2R[:, :], CN["fb2"].ap())),
         wr=[rFB2R], dma="dcg1")

    esB = ExitStack()
    GAT = esB.enter_context(nc.sbuf_tensor("GAT", [128, 4 * 32], F32))
    GA2 = esB.enter_context(nc.sbuf_tensor("GA2", [128, 32], F32))
    GB2 = esB.enter_context(nc.sbuf_tensor("GB2", [128, 4 * 32], F32))
    GC2 = esB.enter_context(nc.sbuf_tensor("GC2", [128, 4 * 32], F32))
    WR1 = esB.enter_context(nc.sbuf_tensor("WR1", [128, 8 * 1024], BF16))
    WR2 = esB.enter_context(nc.sbuf_tensor("WR2", [128, 8 * 8], BF16))
    wload(WR1[:, :].rearrange("p (k m) -> p k m", k=8),
          CN["r1w"].ap().rearrange("(k p) m -> p k m", p=128), rWR1g, "dwr1")
    wload(WR2[:, :].rearrange("p (k m) -> p k m", k=8),
          CN["r2w"].ap().rearrange("(k p) m -> p k m", p=128), rWR2g, "dwr2")
    G1 = esB.enter_context(nc.sbuf_tensor("G1", [128, 8 * 2048], BF16))
    LOG = esB.enter_context(nc.sbuf_tensor("LOG", [8, 2048], F32))

    es0 = ExitStack()
    PT = PBF
    XST = es0.enter_context(nc.sbuf_tensor("XST", [128, 4 * 1024], F32))
    XNT = es0.enter_context(nc.sbuf_tensor("XNT", [128, 4 * 1024], BF16))
    MV4 = es0.enter_context(nc.sbuf_tensor("MV4", [128, 4 * 16], F32))
    if True:
        rPT = rPBF
        rXST = [Res(f"XST{i}") for i in range(4)]
        rXNT = [Res(f"XNT{i}") for i in range(4)]
        rMV4 = [Res(f"MV4{i}") for i in range(4)]

        def p0_stages(tc):
            p4 = tc % 4
            p2 = tc % 2
            xs = XST[:, 1024 * p4: 1024 * p4 + 1024]
            xnt = XNT[:, 1024 * p4: 1024 * p4 + 1024]
            mv = MV4[:, 16 * p4: 16 * p4 + 16]
            rmv = rMV4[p4]

            def u0():
                P.op("sync", (lambda tc=tc, xs=xs: y_.dma_start(
                    xs, xd.ap()[128 * tc: 128 * tc + 128, :])),
                    wr=[rXST[p4]], dma=f"dx{p4}")

            def u1():
                P.op("vector", (lambda: v_.bn_stats(mv[:, 0:6], xs[:, 0:512])),
                     rd=[rXST[p4]], wr=[rmv])
                P.op("vector", (lambda: v_.bn_stats(mv[:, 6:12], xs[:, 512:1024])),
                     rd=[rXST[p4]], wr=[rmv])
                P.op("vector", (lambda: v_.bn_aggr(mv[:, 12:14], mv[:, 0:12])),
                     rd=[rmv], wr=[rmv])

            def u2():
                P.op("scalar", (lambda: s_.activation(mv[:, 14:15], mv[:, 13:14],
                                                      AF.Ln, bias=CS["epsc"][:, 0:1])),
                     rd=[rmv, rCS["epsc"]], wr=[rmv])
                P.op("scalar", (lambda: s_.activation(mv[:, 15:16], mv[:, 14:15],
                                                      AF.Exp, scale=-0.5)),
                     rd=[rmv], wr=[rmv])
            def u3():
                P.op("vector", (lambda: v_.tensor_scalar(
                    xnt, xs, mv[:, 12:13], mv[:, 15:16], ALU.subtract, ALU.mult)),
                    rd=[rXST[p4], rmv], wr=[rXNT[p4]])
                P.op("vector", (lambda tc=tc: v_.tensor_tensor(
                    S[:, 1024 * tc: 1024 * tc + 1024], xs, xnt, ALU.add)),
                    rd=[rXST[p4], rXNT[p4]], wr=[rS[tc]])

            def u4():
                for fc in range(8):
                    P.op("tensor", (lambda fc=fc: t_.transpose(
                        PT[:, 1024 * p2 + 128 * fc: 1024 * p2 + 128 * fc + 128],
                        xnt[:, 128 * fc: 128 * fc + 128], CS["idb"][:, :])),
                        rd=[rXNT[p4], rCS["idb"]], wr=[rPT[p2]])

            def u5():
                P.op("scalar", (lambda tc=tc: s_.copy(
                    XN3[:, :, 128 * tc: 128 * tc + 128],
                    PT[:, 1024 * p2: 1024 * p2 + 1024].rearrange(
                        "p (c x) -> p c x", c=8))),
                    rd=[rPT[p2]], wr=[rXN4[tc // 4]])

            return [u0, u1, u2, u3, u4, u5]

        p0s = [p0_stages(tc) for tc in range(NTC)]
        for tc in range(4):
            p0s[tc][0]()
        emit_const_loads()
        P.op("scalar", (lambda: s_.mul(QBS[:, 0:2], CS["qb1"][:, 0:2], QSCALE)),
             rd=[rCS["qb1"]], wr=[rQBS])
        P.op("scalar", (lambda: s_.mul(QBS[:, 2:4], CS["qb2"][:, 0:2], QSCALE)),
             rd=[rCS["qb2"]], wr=[rQBS])
        for step in range(NTC + 5):
            for sidx in range(6):
                t = step - sidx
                if 0 <= t < NTC and not (sidx == 0 and t < 4):
                    p0s[t][sidx]()
        P.fence()
        P.replay(blk)
    es0.close()

    PA, PB, PS, PD = Fb(0), Fb(1), Fb(2), Fb(3)
    if True:
        rPA, rPB, rPS, rPD = rPF[0], rPF[1], rPF[2], rPF[3]
        rWR1, rWR2, rG1, rLOG = rWR1g, rWR2g, Res("G1"), Res("LOG")
        pbanks = [(PA, rPA), (PB, rPB)]
        for m in range(8):
            for nt in range(NNT):
                ps, rps = pbanks[(m * NNT + nt) % 2]
                blds = [
                    (lambda m=m, nt=nt, k=k, ps=ps: t_.matmul(
                        ps[:, 0:512],
                        WR1[:, 1024 * k + 128 * m: 1024 * k + 128 * m + 128],
                        XN3[:, k, 512 * nt: 512 * nt + 512],
                        start=(k == 0), stop=(k == 7)))
                    for k in range(8)]
                P.op("tensor", blds, rd=[rWR1, rXN4[nt]], wr=[rps])
                P.op("scalar", (lambda m=m, nt=nt, ps=ps: s_.activation(
                    G1[:, 2048 * m + 512 * nt: 2048 * m + 512 * nt + 512],
                    ps[:, 0:512], AF.Silu, bias=CS["rb1c"][:, m: m + 1])),
                    rd=[rps, rCS["rb1c"]], wr=[rG1])
        G13 = G1[:, :].rearrange("p (c x) -> p c x", c=8)
        for nt in range(NNT):
            blds = [
                (lambda nt=nt, k=k: t_.matmul(
                    PS[0:8, 0:512], WR2[:, 8 * k: 8 * k + 8],
                    G13[:, k, 512 * nt: 512 * nt + 512],
                    start=(k == 0), stop=(k == 7)))
                for k in range(8)]
            P.op("tensor", blds, rd=[rWR2, rG1], wr=[rPS])
            P.op("scalar", (lambda nt=nt: s_.activation(
                LOG[0:8, 512 * nt: 512 * nt + 512], PS[0:8, 0:512],
                AF.Identity, bias=CS["rb2c"][0:8, 0:1])),
                rd=[rPS, rCS["rb2c"]], wr=[rLOG])
        rGT = Res("GT")
        PDB = [PA, PB, PS, PD]
        rPD4 = [rPF[0], rPF[1], rPF[2], rPF[3]]
        for tc in range(NTC):
            p4 = tc % 4
            pd = PDB[p4][:, 0:128]
            P.op("tensor", (lambda tc=tc, pd=pd: t_.transpose(
                pd[:, 0:8], LOG[0:8, 128 * tc: 128 * tc + 128],
                CS["idf"][0:8, 0:8])),
                rd=[rLOG, rCS["idf"]], wr=[rPD4[p4]])
            P.op("vector", (lambda tc=tc, pd=pd: v_.tensor_copy(
                LTM[:, 8 * tc: 8 * tc + 8], pd[:, 0:8])),
                rd=[rPD4[p4]], wr=[rLTM])
        for q in range(4):
            lt = LTM[:, 32 * q: 32 * q + 32]
            lt3 = lt.rearrange("p (c j) -> p c j", c=4)
            P.op("vector", (lambda lt3=lt3, q=q: v_.reduce_max(
                GA2[:, 8 * q: 8 * q + 4], lt3, axis=AX.X)),
                rd=[rLTM], wr=[rGT])
            P.op("vector", (lambda lt3=lt3, q=q: v_.reduce_max(
                GA2[:, 8 * q + 4: 8 * q + 8], lt3, axis=AX.X, negate=True)),
                rd=[rLTM], wr=[rGT])
            negx = bass.AP(GA2, 8 * q + 4, [[32, 128], [1, 4], [0, 8]])
            m1xx = bass.AP(GA2, 8 * q, [[32, 128], [1, 4], [0, 8]])
            P.op("vector", (lambda lt3=lt3, negx=negx, q=q: v_.tensor_tensor(
                GB2[:, 32 * q: 32 * q + 32].rearrange("p (c j) -> p c j", c=4),
                lt3, negx, ALU.add)),
                rd=[rLTM, rGT], wr=[rGT])
            P.op("scalar", (lambda q=q: s_.activation(
                GAT[:, 32 * q: 32 * q + 32], GB2[:, 32 * q: 32 * q + 32],
                AF.Exp)),
                rd=[rGT], wr=[rGT])
            P.op("vector", (lambda lt3=lt3, m1xx=m1xx, q=q: v_.tensor_tensor(
                GC2[:, 32 * q: 32 * q + 32].rearrange("p (c j) -> p c j", c=4),
                lt3, m1xx, ALU.is_ge)),
                rd=[rLTM, rGT], wr=[rGT])
            P.op("vector", (lambda lt=lt, q=q: v_.scalar_tensor_tensor(
                GC2[:, 32 * q: 32 * q + 32], GC2[:, 32 * q: 32 * q + 32],
                -1e9, lt, ALU.mult, ALU.add)),
                rd=[rLTM, rGT], wr=[rGT])
            P.op("vector", (lambda q=q: v_.reduce_max(
                GA2[:, 8 * q: 8 * q + 4],
                GC2[:, 32 * q: 32 * q + 32].rearrange("p (c j) -> p c j", c=4),
                axis=AX.X)),
                rd=[rGT], wr=[rGT])
            P.op("vector", (lambda lt3=lt3, m1xx=m1xx, q=q: v_.tensor_tensor(
                GC2[:, 32 * q: 32 * q + 32].rearrange("p (c j) -> p c j", c=4),
                lt3, m1xx, ALU.is_ge)),
                rd=[rLTM, rGT], wr=[rGT])
            P.op("vector", (lambda q=q: v_.tensor_tensor(
                GAT[:, 32 * q: 32 * q + 32], GAT[:, 32 * q: 32 * q + 32],
                GC2[:, 32 * q: 32 * q + 32], ALU.mult)),
                rd=[rGT], wr=[rGT])
            P.op("vector", (lambda q=q: v_.reduce_sum(
                GA2[:, 8 * q: 8 * q + 4],
                GAT[:, 32 * q: 32 * q + 32].rearrange("p (c j) -> p c j", c=4),
                axis=AX.X)),
                rd=[rGT], wr=[rGT])
            P.op("vector", (lambda q=q: v_.reciprocal(
                GA2[:, 8 * q + 4: 8 * q + 8], GA2[:, 8 * q: 8 * q + 4])),
                rd=[rGT], wr=[rGT])
            rx = bass.AP(GA2, 8 * q + 4, [[32, 128], [1, 4], [0, 8]])
            P.op("vector", (lambda rx=rx, q=q: v_.tensor_tensor(
                WTM[:, 32 * q: 32 * q + 32].rearrange("p (c j) -> p c j", c=4),
                GAT[:, 32 * q: 32 * q + 32].rearrange("p (c j) -> p c j", c=4),
                rx, ALU.mult)),
                rd=[rGT], wr=[rWTM])
            for tc in range(4 * q, 4 * q + 4):
                p4 = tc % 4
                pd = PDB[p4][:, 0:128]
                P.op("tensor", (lambda tc=tc, pd=pd: t_.transpose(
                    pd[0:8, 0:128], WTM[:, 8 * tc: 8 * tc + 8], CS["idf"][:, :])),
                    rd=[rWTM, rCS["idf"]], wr=[rPD4[p4]])
                P.op("scalar", (lambda tc=tc, pd=pd: s_.copy(
                    WFM[0:8, 128 * tc: 128 * tc + 128], pd[0:8, 0:128])),
                    rd=[rPD4[p4]], wr=[rWFM])

        P.fence()
        P.replay(blk)
    esB.close()

    esU = ExitStack()
    UPIN = esU.enter_context(nc.sbuf_tensor("UPIN", [128, 4 * 2048], BF16))
    WUP = esU.enter_context(nc.sbuf_tensor("WUP", [128, 4 * 1024], BF16))
    if True:
        rUPIN = [Res(f"UPIN{i}") for i in range(NNT)]
        rWUP = Res("WUP")
        wload(WUP[:, :].rearrange("p (k m) -> p k m", k=4),
              CN["uw"].ap().rearrange("(k p) m -> p k m", p=128), rWUP, "dwup")
        UPIN3 = UPIN[:, :].rearrange("p (c x) -> p c x", c=4)

        def upin_res(lo, hi):
            return [rUPIN[i] for i in range(NNT) if lo < 512 * (i + 1) and hi > 512 * i]

        esA = ExitStack()
        XI = esA.enter_context(nc.sbuf_tensor("XI", [128, 4 * 2052], BF16))
        QKV = esA.enter_context(nc.sbuf_tensor("QKV", [128, 4 * 2052], BF16))
        VT17 = esA.enter_context(nc.sbuf_tensor("VT17", [128, 17 * 256], BF16))
        Z = esA.enter_context(nc.sbuf_tensor("Z", [128, 4 * 960], F32))
        E = esA.enter_context(nc.sbuf_tensor("E", [128, 4 * 960], BF16))
        PTS = esA.enter_context(nc.sbuf_tensor("PTS", [128, 2 * 1024], BF16))
        AVS = esA.enter_context(nc.sbuf_tensor("AVS", [128, 2 * 240], BF16))
        NEG = esA.enter_context(nc.sbuf_tensor("NEG", [128, 4 * 8], F32))
        SUM = esA.enter_context(nc.sbuf_tensor("SUMS", [128, 4 * 8], F32))
        RCP = esA.enter_context(nc.sbuf_tensor("RCPS", [128, 4 * 8], F32))
        SQ = esA.enter_context(nc.sbuf_tensor("SQ", [128, 4 * 512], BF16))
        XS = esA.enter_context(nc.sbuf_tensor("XS", [128, 2 * 16], BF16))
        QKVS = esA.enter_context(nc.sbuf_tensor("QKVS", [128, 4 * 16], BF16))
        VS = esA.enter_context(nc.sbuf_tensor("VS", [128, 256], BF16))
        RSTD = esA.enter_context(nc.sbuf_tensor("RSTD", [1, 512], F32))
        WQ1 = esA.enter_context(nc.sbuf_tensor("WQ1", [128, 2 * 768], BF16))
        WQ2 = esA.enter_context(nc.sbuf_tensor("WQ2", [128, 2 * 768], BF16))
        WPJ1 = esA.enter_context(nc.sbuf_tensor("WPJ1", [128, 2 * 256], BF16))
        WPJ2 = esA.enter_context(nc.sbuf_tensor("WPJ2", [128, 2 * 256], BF16))
        for n in BRANCH_CONSTS:
            shp, dt = CSHAPES[n]
            CS[n] = esA.enter_context(nc.sbuf_tensor("c_" + n, shp, dt))
            P.op("sync", (lambda d=CS[n], a=CN[n].ap(): y_.dma_start(d[:, :], a)),
                 wr=[rCS[n]], dma="dc_" + n)
        if True:
            rXI, rQKV = Res("XI"), Res("QKV")
            rVT17 = Res("VT17")
            rZ = [Res(f"Z{i}") for i in range(4)]
            rE = [Res(f"E{i}") for i in range(4)]
            rPTS = [Res("PTS0"), Res("PTS1")]
            rAVS = [Res("AVS0"), Res("AVS1")]
            rNEGL = [Res(f"NEG{i}") for i in range(4)]
            rSUML = [Res(f"SUM{i}") for i in range(4)]
            rRCPL = [Res(f"RCP{i}") for i in range(4)]
            rSQ = Res("SQ")
            rXS, rQKVS, rVS = Res("XS"), Res("QKVS"), Res("VS")
            rWQ1, rWQ2, rWPJ1, rWPJ2 = Res("WQ1"), Res("WQ2"), Res("WP1"), Res("WP2")
            wload(WQ1[:, :].rearrange("p (k m) -> p k m", k=2),
                  CN["q1w"].ap().rearrange("(k p) m -> p k m", p=128), rWQ1, "dwq1")
            wload(WQ2[:, :].rearrange("p (k m) -> p k m", k=2),
                  CN["q2w"].ap().rearrange("(k p) m -> p k m", p=128), rWQ2, "dwq2")
            wload(WPJ1[:, :].rearrange("p (k m) -> p k m", k=2),
                  CN["p1w"].ap().rearrange("(k p) m -> p k m", p=128), rWPJ1, "dwp1")
            wload(WPJ2[:, :].rearrange("p (k m) -> p k m", k=2),
                  CN["p2w"].ap().rearrange("(k p) m -> p k m", p=128), rWPJ2, "dwp2")
            XI3 = XI[:, :].rearrange("p (c x) -> p c x", c=4)
            QKV3 = QKV[:, :].rearrange("p (c x) -> p c x", c=4)

            PA, PB, PC, PS = Fb(0), Fb(1), Fb(2), Fb(3)
            if True:
                rPA, rPB, rPC, rPS = rPF[0], rPF[1], rPF[2], rPF[3]
                pbanks = [(PA, rPA), (PB, rPB)]
                for nt in range(NNT):
                    for m in range(4):
                        ps, rps = pbanks[m % 2]
                        blds = [
                            (lambda m=m, nt=nt, k=k, ps=ps: t_.matmul(
                                ps[:, 0:512],
                                WDOWN[:, 512 * k + 128 * m: 512 * k + 128 * m + 128],
                                XN3[:, k, 512 * nt: 512 * nt + 512],
                                start=(k == 0), stop=(k == 7)))
                            for k in range(8)]
                        P.op("tensor", blds, rd=[rWDOWN, rXN4[nt]], wr=[rps])
                        P.op("scalar", (lambda m=m, nt=nt, ps=ps: s_.activation(
                            XI3[:, m, 512 * nt: 512 * nt + 512], ps[:, 0:512],
                            AF.Identity, bias=CS["dbc"][:, m: m + 1])),
                            rd=[rps, rCS["dbc"]], wr=[rXI])
                for c in range(4):
                    P.op("vector", (lambda c=c: v_.memset(XI3[:, c, 2048:2052], 0.0)),
                         wr=[rXI])
                BCB = [(PC, rPC), (PA, rPA)]

                def xi_sq(nt):
                    sl = 1024 * (nt % 2)
                    for c in range(4):
                        sq = SQ[:, sl + 512 * (c % 2): sl + 512 * (c % 2) + 512]
                        P.op("vector", (lambda c=c, nt=nt, sq=sq: v_.tensor_tensor(
                            sq, XI3[:, c, 512 * nt: 512 * nt + 512],
                            XI3[:, c, 512 * nt: 512 * nt + 512], ALU.mult)),
                            rd=[rXI], wr=[rSQ])
                    for c in range(4):
                        sq = SQ[:, sl + 512 * (c % 2): sl + 512 * (c % 2) + 512]
                        P.op("tensor", (lambda c=c, sq=sq: t_.matmul(
                            PS[0:1, 0:512], CS["onesb"][:, 0:1], sq,
                            start=(c == 0), stop=(c == 3))),
                            rd=[rSQ, rCS["onesb"]], wr=[rPS])
                    P.op("scalar", (lambda nt=nt: s_.activation(
                        RSTD[0:1, :], PS[0:1, 0:512], AF.Ln,
                        bias=CS["epsc"][0:1, 1:2], scale=1.0 / 512)),
                        rd=[rPS, rCS["epsc"]], wr=[rRSTD])
                    P.op("scalar", (lambda: s_.activation(
                        RSTD[0:1, :], RSTD[0:1, :], AF.Exp, scale=-0.5)),
                        rd=[rRSTD], wr=[rRSTD])
                    pcb, rpcb = BCB[nt % 2]
                    P.op("tensor", (lambda pcb=pcb: t_.matmul(
                        pcb[:, 0:512], CS["onesf"][0:1, 0:128], RSTD[0:1, 0:512],
                        start=True, stop=True)),
                        rd=[rRSTD, rCS["onesf"]], wr=[rpcb])

                def xi_norm(nt):
                    pcb, rpcb = BCB[nt % 2]
                    for c in range(4):
                        P.op("vector", (lambda c=c, nt=nt, pcb=pcb:
                             v_.scalar_tensor_tensor(
                                 XI3[:, c, 512 * nt: 512 * nt + 512], pcb[:, 0:512],
                                 CS["n1c"][:, c: c + 1],
                                 XI3[:, c, 512 * nt: 512 * nt + 512],
                                 ALU.mult, ALU.mult)),
                            rd=[rpcb, rCS["n1c"], rXI], wr=[rXI])

                xi_sq(0)
                for nt in range(NNT):
                    if nt + 1 < NNT:
                        xi_sq(nt + 1)
                    xi_norm(nt)
                P.replay(blk)

            SCP = PF[:, 0:2048]
            PT = PBF
            PE2 = PF[:, 2048:3072]
            if True:
                rSCB = [rPF[0], rPF[1], rPF[2], rPF[3]]
                rPT = rPBF
                rPEa, rPEb = rPF[4], rPF[5]
                SCP4 = SCP[:, :].rearrange("p (a b x) -> p a b x", a=4, b=4)
                PE2s = PE2[:, :].rearrange("p (b x) -> p b x", b=8)

                def do_tile(br, n, p4, p2, q_ap, k_ap, vdst, rvdst,
                            bias_ap, rd_bias, amask_ap, out_map, rd_qkv,
                            WPJ, rWPJ, PBc, rPBc):
                    nj = n
                    zsl = Z[0:nj, 960 * p4: 960 * p4 + 8 * nj]
                    zv = zsl.rearrange("p (j x) -> p j x", j=8)
                    esl = E[0:nj, 960 * p4: 960 * p4 + 8 * nj]
                    ev = esl.rearrange("p (j x) -> p j x", j=8)

                    def s0():
                        for h in range(8):
                            a, b = h % 4, h // 4
                            P.op("tensor", (lambda h=h, a=a, b=b, nj=nj: t_.matmul(
                                SCP[0:nj, 512 * a + 128 * b: 512 * a + 128 * b + nj],
                                q_ap(h), k_ap(h), start=True, stop=True,
                                tile_position=(32 * a, 0))),
                                rd=rd_qkv, wr=[rSCB[a]])
                        P.op("vector", (lambda nj=nj, bias_ap=bias_ap:
                             v_.tensor_tensor(
                                 zsl, SCP4[0:nj, :, 0:2, 0:nj], bias_ap, ALU.add)),
                            rd=rSCB + rd_bias, wr=[rZ[p4]])
                        if amask_ap is not None:
                            P.op("vector", (lambda: v_.tensor_tensor(
                                zsl, zsl, amask_ap, ALU.add)),
                                rd=[rZ[p4], rCS["am1"]], wr=[rZ[p4]])

                    def s1():
                        P.op("vector", (lambda: v_.reduce_max(
                            NEG[0:nj, 8 * p4: 8 * p4 + 8], zv,
                            axis=AX.X, negate=True)),
                            rd=[rZ[p4]], wr=[rNEGL[p4]])
                        negb = bass.AP(NEG, 8 * p4, [[32, nj], [1, 8], [0, nj]])
                        P.op("vector", (lambda negb=negb: v_.tensor_tensor(
                            zv, zv, negb, ALU.add)),
                            rd=[rZ[p4], rNEGL[p4]], wr=[rZ[p4]])

                    def s2():
                        P.op("scalar", (lambda: s_.activation(esl, zsl, AF.Exp)),
                             rd=[rZ[p4]], wr=[rE[p4]])

                    def s3():
                        P.op("vector", (lambda: v_.reduce_sum(
                            SUM[0:nj, 8 * p4: 8 * p4 + 8], ev, axis=AX.X)),
                            rd=[rE[p4]], wr=[rSUML[p4]])
                        P.op("vector", (lambda: v_.reciprocal(
                            RCP[0:nj, 8 * p4: 8 * p4 + 8],
                            SUM[0:nj, 8 * p4: 8 * p4 + 8])),
                            rd=[rSUML[p4]], wr=[rRCPL[p4]])
                        rcpb = bass.AP(RCP, 8 * p4, [[32, nj], [1, 8], [0, nj]])
                        P.op("vector", (lambda rcpb=rcpb: v_.tensor_tensor(
                            ev, ev, rcpb, ALU.mult)),
                            rd=[rRCPL[p4], rE[p4]], wr=[rE[p4]])

                    def s4():
                        for j in range(8):
                            P.op("tensor", (lambda j=j: t_.transpose(
                                PT[0:nj, 1024 * p2 + 128 * j: 1024 * p2 + 128 * j + nj],
                                E[0:nj, 960 * p4 + nj * j: 960 * p4 + nj * j + nj],
                                CS["idb"][0:nj, 0:nj])),
                                rd=[rE[p4], rCS["idb"]], wr=[rPT[p2]])
                        P.op("scalar", (lambda: s_.copy(
                            PTS[0:nj, 1024 * p2: 1024 * p2 + 8 * nj],
                            PT[0:nj, 1024 * p2: 1024 * p2 + 1024].rearrange(
                                "p (j x) -> p j x", j=8)[:, :, 0:nj])),
                            rd=[rPT[p2]], wr=[rPTS[p2]])

                    def s5():
                        for j in range(8):
                            h = JORDER[j]
                            a, b = h % 4, h // 4
                            P.op("tensor", (lambda j=j, h=h, a=a, b=b: t_.matmul(
                                PE2[32 * a: 32 * a + 32, 128 * b: 128 * b + nj],
                                vdst[0:nj, 32 * h: 32 * h + 32],
                                PTS[0:nj, 1024 * p2 + nj * j: 1024 * p2 + nj * j + nj],
                                start=True, stop=True, tile_position=(0, 32 * a))),
                                rd=[rvdst, rPTS[p2]], wr=[rPEa])
                        P.op("scalar", (lambda: s_.copy(
                            AVS[:, 240 * p2: 240 * p2 + 2 * nj], PE2s[:, 0:2, 0:nj])),
                            rd=[rPEa], wr=[rAVS[p2]])

                    def s6():
                        for mc in range(2):
                            pcol = 256 if mc == 0 else 640
                            rpe = rPEa if mc == 0 else rPEb
                            blds = [
                                (lambda kc=kc, mc=mc, pcol=pcol: t_.matmul(
                                    PE2[0:128, pcol: pcol + nj],
                                    WPJ[:, 256 * kc + 128 * mc:
                                        256 * kc + 128 * mc + 128],
                                    AVS[:, 240 * p2 + nj * kc: 240 * p2 + nj * kc + nj],
                                    start=(kc == 0), stop=(kc == 1)))
                                for kc in range(2)]
                            P.op("tensor", blds, rd=[rWPJ, rAVS[p2]], wr=[rpe])
                            for (plo, wdt, ucol) in out_map:
                                P.op("scalar", (lambda plo=plo, wdt=wdt, ucol=ucol,
                                                mc=mc, br=br, pcol=pcol: s_.activation(
                                    UPIN3[:, 2 * br + mc, ucol: ucol + wdt],
                                    PE2[0:128, pcol + plo: pcol + plo + wdt],
                                    AF.Identity, bias=PBc[:, mc: mc + 1])),
                                    rd=[rpe, rPBc], wr=upin_res(ucol, ucol + wdt))

                    return [s0, s1, s2, (lambda: None), s3, s4, s5, s6]

                for br in range(2):
                    WQ, rWQ = (WQ1, rWQ1) if br == 0 else (WQ2, rWQ2)
                    WPJ, rWPJ = (WPJ1, rWPJ1) if br == 0 else (WPJ2, rWPJ2)
                    PBc, rPBc = ((CS["pb1"], rCS["pb1"]) if br == 0
                                 else (CS["pb2"], rCS["pb2"]))
                    vb, rvb = ((CS["vb1"], rCS["vb1"]) if br == 0
                               else (CS["vb2"], rCS["vb2"]))
                    qb, rqb = ((CS["qb1"], rCS["qb1"]) if br == 0
                               else (CS["qb2"], rCS["qb2"]))
                    btn = "bt1" if br == 0 else "bt2"
                    off = 5 if br == 0 else 10
                    ntiles_n = [(nt, 512 * nt, 512) for nt in range(NNT)]
                    if br == 1:
                        ntiles_n.append((0, 2048, 4))
                    for m in range(4):
                        for (bidx, col0, wdt) in ntiles_n:
                            bank = (bidx if wdt == 512 else 3)
                            ps_ap = SCP[:, 512 * bank: 512 * bank + wdt]
                            rps = rSCB[bank]
                            blds = [
                                (lambda m=m, k=k, col0=col0, wdt=wdt, ps_ap=ps_ap,
                                 br=br, WQ=WQ: t_.matmul(
                                     ps_ap,
                                     WQ[:, 768 * k + 128 * m: 768 * k + 128 * m + 128],
                                     XI3[:, 2 * br + k, col0: col0 + wdt],
                                     start=(k == 0), stop=(k == 1)))
                                for k in range(2)]
                            P.op("tensor", blds, rd=[rWQ, rXI], wr=[rps])
                            bias = (QBS[:, 2 * br + m: 2 * br + m + 1] if m < 2
                                    else qb[:, m: m + 1])
                            scl = QSCALE if m < 2 else 1.0
                            rb = rQBS if m < 2 else rqb
                            P.op("scalar", (lambda m=m, col0=col0, wdt=wdt,
                                            ps_ap=ps_ap, bias=bias, scl=scl:
                                 s_.activation(
                                     QKV3[:, m, col0: col0 + wdt], ps_ap,
                                     AF.Identity, bias=bias, scale=scl)),
                                rd=[rps, rb], wr=[rQKV])
                    for t in range(17):
                        tok0 = off + 120 * t
                        pvs = PE2[0:120, 0:256] if t % 2 == 0 else PE2[0:120, 512:768]
                        rpv = rPEa if t % 2 == 0 else rPEb
                        blds = [
                            (lambda k=k, tok0=tok0, br=br, WQ=WQ, pvs=pvs: t_.matmul(
                                pvs, XI3[:, 2 * br + k, tok0: tok0 + 120],
                                WQ[:, 768 * k + 512: 768 * k + 768],
                                start=(k == 0), stop=(k == 1)))
                            for k in range(2)]
                        P.op("tensor", blds, rd=[rXI, rWQ], wr=[rpv])
                        P.op("vector", (lambda t=t, pvs=pvs, vb=vb: v_.tensor_tensor(
                            VT17[0:120, 256 * t: 256 * t + 256], pvs,
                            vb[0:120, 0:256], ALU.add)),
                            rd=[rpv, rvb], wr=[rVT17])
                    stages = []
                    for t in range(17):
                        tok0 = off + 120 * t
                        wdt = 120 if (br == 0 or t < 16) else 118
                        q_ap = (lambda h, tok0=tok0: QKV3[
                            32 * (h % 4): 32 * (h % 4) + 32, h // 4, tok0: tok0 + 120])
                        k_ap = (lambda h, tok0=tok0: QKV3[
                            32 * (h % 4): 32 * (h % 4) + 32, 2 + h // 4,
                            tok0: tok0 + 120])
                        stages.append(do_tile(br, 120, t % 4, t % 2, q_ap, k_ap,
                                      VT17[:, 256 * t: 256 * t + 256], rVT17,
                                      CS[btn][0:120, 0:960], [rCS[btn]], None,
                                      [(0, wdt, tok0)], [rQKV],
                                      WPJ, rWPJ, PBc, rPBc))
                    NSTG = 8
                    for step in range(17 + NSTG - 1):
                        for sidx in range(NSTG):
                            t = step - sidx
                            if 0 <= t < 17:
                                stages[t][sidx]()
                    if br == 0:
                        for c in range(2):
                            P.op("vector", (lambda c=c: v_.tensor_copy(
                                XS[:, 16 * c: 16 * c + 3], XI3[:, c, 2045: 2048])),
                                rd=[rXI], wr=[rXS])
                            P.op("vector", (lambda c=c: v_.memset(
                                XS[:, 16 * c + 3: 16 * c + 5], 0.0)), wr=[rXS])
                            P.op("vector", (lambda c=c: v_.tensor_copy(
                                XS[:, 16 * c + 5: 16 * c + 10], XI3[:, c, 0: 5])),
                                rd=[rXI], wr=[rXS])
                        sq_src = (lambda k: XS[:, 16 * k: 16 * k + 10])
                        rd_src = [rXS]
                        out_map = [(0, 3, 2045), (5, 5, 0)]
                        bias_ap = CS["bs1"][0:10, 0:80]
                        rd_bias = [rCS["bs1"]]
                        amask_ap = CS["am1"][0:10, 0:80]
                    else:
                        sq_src = (lambda k: XI3[:, 2 + k, 0: 10])
                        rd_src = [rXI]
                        out_map = [(0, 10, 0)]
                        bias_ap = CS["bs2"][0:10, 0:80]
                        rd_bias = [rCS["bs2"]]
                        amask_ap = None
                    for m in range(4):
                        blds = [
                            (lambda m=m, k=k, WQ=WQ, sq_src=sq_src: t_.matmul(
                                SCP[0:128, 1536 + 16 * m: 1536 + 16 * m + 10],
                                WQ[:, 768 * k + 128 * m: 768 * k + 128 * m + 128],
                                sq_src(k), start=(k == 0), stop=(k == 1)))
                            for k in range(2)]
                        P.op("tensor", blds, rd=rd_src + [rWQ], wr=[rSCB[3]])
                        bias = (QBS[:, 2 * br + m: 2 * br + m + 1] if m < 2
                                else qb[:, m: m + 1])
                        scl = QSCALE if m < 2 else 1.0
                        rb = rQBS if m < 2 else rqb
                        P.op("scalar", (lambda m=m, bias=bias, scl=scl: s_.activation(
                            QKVS[:, 16 * m: 16 * m + 10],
                            SCP[0:128, 1536 + 16 * m: 1536 + 16 * m + 10],
                            AF.Identity, bias=bias, scale=scl)),
                            rd=[rSCB[3], rb], wr=[rQKVS])
                    q_ap = (lambda h: QKVS[32 * (h % 4): 32 * (h % 4) + 32,
                                           16 * (h // 4): 16 * (h // 4) + 10])
                    k_ap = (lambda h: QKVS[32 * (h % 4): 32 * (h % 4) + 32,
                                           16 * (2 + h // 4): 16 * (2 + h // 4) + 10])
                    v_blds = [
                        (lambda k=k, WQ=WQ, sq_src=sq_src: t_.matmul(
                            PE2[0:10, 0:256], sq_src(k),
                            WQ[:, 768 * k + 512: 768 * k + 768],
                            start=(k == 0), stop=(k == 1)))
                        for k in range(2)]
                    P.op("tensor", v_blds, rd=rd_src + [rWQ], wr=[rPEa])
                    P.op("vector", (lambda vb=vb: v_.tensor_tensor(
                        VS[0:10, :], PE2[0:10, 0:256], vb[0:10, 0:256], ALU.add)),
                        rd=[rPEa, rvb], wr=[rVS])
                    for st in do_tile(br, 10, 0, 0, q_ap, k_ap,
                                      VS, rVS, bias_ap, rd_bias, amask_ap,
                                      out_map, [rQKVS],
                                      WPJ, rWPJ, PBc, rPBc):
                        st()
                P.replay(blk)
        esA.close()
        P.fence()

        esA4 = ExitStack()
        H = esA4.enter_context(nc.sbuf_tensor("H", [128, 8 * 1024], BF16))
        MID = esA4.enter_context(nc.sbuf_tensor("MID", [128, 16 * 512], BF16))
        RST2 = esA4.enter_context(nc.sbuf_tensor("RST2", [1, 2 * 512], F32))
        WF1 = esA4.enter_context(nc.sbuf_tensor("WF1", [128, 8 * 2048], BF16))
        WF2 = esA4.enter_context(nc.sbuf_tensor("WF2", [128, 16 * 1024], BF16))
        PA, PB, PC = Fb(0), Fb(1), Fb(2)
        PS = PF[:, 1024:1536]
        PT = PBF
        PE2 = PF[:, 2048:3072]
        PE2H = Fb(3)
        if True:
            rH = [Res("H0"), Res("H1")]
            rMID = Res("MID")
            rRST2 = Res("RST2")
            rWF1, rWF2 = Res("WF1"), Res("WF2")
            rPA, rPB, rPC, rPS = rPF[0], rPF[1], rPF[2], rPF[2]
            rPT = rPBF
            rPE2 = rPF[4]
            rPE2F5 = rPF[5]
            rPE2H = rPF[3]
            wload(WF1[:, :].rearrange("p (k m) -> p k m", k=8),
                  CN["f1w"].ap().rearrange("(k p) m -> p k m", p=128), rWF1, "dwf1")
            wload(WF2[:, :].rearrange("p (k m) -> p k m", k=16),
                  CN["f2w"].ap().rearrange("(k p) m -> p k m", p=128), rWF2, "dwf2")
            MID3 = MID[:, :].rearrange("p (c x) -> p c x", c=16)
            XA3L = [MID3[:, 8:16, :], MID3[:, 0:8, :]]
            rXAL = [rMID, rMID]
            H3 = H[:, :].rearrange("p (c x) -> p c x", c=8)
            pbanks = [(PA, rPA), (PB, rPB)]

            def a4_up(nt):
                hs = nt % 2
                XA3 = XA3L[hs]
                for m in range(8):
                    ps, rps = pbanks[m % 2]
                    blds = [
                        (lambda m=m, nt=nt, k=k, ps=ps: t_.matmul(
                            ps[:, 0:512],
                            WUP[:, 1024 * k + 128 * m: 1024 * k + 128 * m + 128],
                            UPIN3[:, k, 512 * nt: 512 * nt + 512],
                            start=(k == 0), stop=(k == 3)))
                        for k in range(4)]
                    P.op("tensor", blds,
                         rd=[rWUP] + upin_res(512 * nt, 512 * nt + 512), wr=[rps])
                    P.op("vector", (lambda m=m, nt=nt, ps=ps, XA3=XA3:
                         v_.scalar_tensor_tensor(
                             XA3[:, m, 0:512], ps[:, 0:512], CS["ubc"][:, m: m + 1],
                             XN3[:, m, 512 * nt: 512 * nt + 512], ALU.add, ALU.add)),
                        rd=[rps, rCS["ubc"], rXN4[nt]], wr=[rXAL[hs]])

            def a4_trans(nt):
                hs = nt % 2
                XA3 = XA3L[hs]
                for tcl in range(4):
                    tc = 4 * nt + tcl
                    pp = tcl % 2
                    for c in range(8):
                        P.op("tensor", (lambda c=c, tcl=tcl, pp=pp, XA3=XA3:
                             t_.transpose(
                                 PT[:, 1024 * pp + 128 * c: 1024 * pp + 128 * c + 128],
                                 XA3[:, c, 128 * tcl: 128 * tcl + 128],
                                 CS["idb"][:, :])),
                            rd=[rXAL[hs], rCS["idb"]], wr=[rPT[pp]])
                    P.op("vector", (lambda tc=tc, pp=pp: v_.tensor_tensor(
                        S[:, 1024 * tc: 1024 * tc + 1024],
                        PT[:, 1024 * pp: 1024 * pp + 1024],
                        S[:, 1024 * tc: 1024 * tc + 1024], ALU.add)),
                        rd=[rPT[pp]], wr=[rS[tc]])

            def a4_sq(nt):
                hs = nt % 2
                XA3 = XA3L[hs]
                for c in range(8):
                    sq = H3[:, c, 512 * (1 - hs): 512 * (1 - hs) + 512]
                    P.op("vector", (lambda c=c, sq=sq, XA3=XA3: v_.tensor_tensor(
                        sq, XA3[:, c, 0:512], XA3[:, c, 0:512], ALU.mult)),
                        rd=[rXAL[hs]], wr=[rH[1 - hs]])

            def a4_stats(nt):
                hs = nt % 2
                for c in range(8):
                    sq = H3[:, c, 512 * (1 - hs): 512 * (1 - hs) + 512]
                    P.op("tensor", (lambda c=c, sq=sq: t_.matmul(
                        PS[0:1, 0:512], CS["onesb"][:, 0:1], sq,
                        start=(c == 0), stop=(c == 7))),
                        rd=[rH[1 - hs], rCS["onesb"]], wr=[rPS])
                P.op("scalar", (lambda hs=hs: s_.activation(
                    RST2[0:1, 512 * hs: 512 * hs + 512], PS[0:1, 0:512], AF.Ln,
                    bias=CS["epsc"][0:1, 1:2], scale=1.0 / 1024)),
                    rd=[rPS, rCS["epsc"]], wr=[rRST2])
                P.op("scalar", (lambda hs=hs: s_.activation(
                    RST2[0:1, 512 * hs: 512 * hs + 512],
                    RST2[0:1, 512 * hs: 512 * hs + 512], AF.Exp, scale=-0.5)),
                    rd=[rRST2], wr=[rRST2])

            def a4_h(nt):
                hs = nt % 2
                XA3 = XA3L[hs]
                P.op("tensor", (lambda hs=hs: t_.matmul(
                    PC[:, 0:512], CS["onesf"][0:1, 0:128],
                    RST2[0:1, 512 * hs: 512 * hs + 512],
                    start=True, stop=True)),
                    rd=[rRST2, rCS["onesf"]], wr=[rPC])
                for c in range(8):
                    P.op("vector", (lambda c=c, hs=hs, XA3=XA3:
                         v_.scalar_tensor_tensor(
                             H3[:, c, 512 * hs: 512 * hs + 512], PC[:, 0:512],
                             CS["n2c"][:, c: c + 1],
                             XA3[:, c, 0:512], ALU.mult, ALU.mult)),
                        rd=[rPC, rCS["n2c"], rXAL[hs]], wr=[rH[hs]])

            def a4_ffn1(nt):
                hs = nt % 2
                for m in range(16):
                    ps, rps = pbanks[m % 2]
                    blds = [
                        (lambda m=m, k=k, ps=ps, hs=hs: t_.matmul(
                            ps[:, 0:512],
                            WF1[:, 2048 * k + 128 * m: 2048 * k + 128 * m + 128],
                            H3[:, k, 512 * hs: 512 * hs + 512],
                            start=(k == 0), stop=(k == 7)))
                        for k in range(8)]
                    P.op("tensor", blds, rd=[rWF1, rH[hs]], wr=[rps])
                    P.op("scalar", (lambda m=m, ps=ps: s_.activation(
                        MID3[:, m, 0:512], ps[:, 0:512], AF.Gelu,
                        bias=CS["fb1c"][:, m: m + 1])),
                        rd=[rps, rCS["fb1c"]], wr=[rMID])

            def a4_ffn2(nt):
                for tcl in range(4):
                    tc = 4 * nt + tcl
                    if tcl % 2 == 0:
                        halves = [(PE2[0:128, 0:512], rPE2),
                                  (PE2[0:128, 512:1024], rPE2F5)]
                    else:
                        halves = [(PE2H[0:128, 0:512], rPE2H),
                                  (PC[0:128, 0:512], rPC)]
                    blds = []
                    for h2 in range(2):
                        ph = halves[h2][0]
                        for k in range(16):
                            blds.append(lambda k=k, h2=h2, tcl=tcl, ph=ph: t_.matmul(
                                ph, MID3[:, k, 128 * tcl: 128 * tcl + 128],
                                WF2[:, 1024 * k + 512 * h2: 1024 * k + 512 * h2 + 512],
                                start=(k == 0), stop=False))
                        blds.append(lambda h2=h2, ph=ph: t_.matmul(
                            ph, CS["onesb"][0:1, 0:128],
                            FB2R[0:1, 512 * h2: 512 * h2 + 512],
                            start=False, stop=True))
                    P.op("tensor", blds,
                         rd=[rMID, rWF2, rFB2R, rCS["onesb"]],
                         wr=[halves[0][1], halves[1][1]])
                    if tcl % 2 == 0:
                        P.op("vector", (lambda tc=tc: v_.tensor_tensor(
                            S[:, 1024 * tc: 1024 * tc + 1024], PE2[0:128, 0:1024],
                            S[:, 1024 * tc: 1024 * tc + 1024], ALU.add)),
                            rd=[rPE2, rPE2F5], wr=[rS[tc]])
                    else:
                        for h2 in range(2):
                            ph, rph = halves[h2]
                            P.op("vector", (lambda tc=tc, h2=h2, ph=ph:
                                 v_.tensor_tensor(
                                     S[:, 1024 * tc + 512 * h2:
                                       1024 * tc + 512 * h2 + 512],
                                     ph,
                                     S[:, 1024 * tc + 512 * h2:
                                       1024 * tc + 512 * h2 + 512],
                                     ALU.add)),
                                rd=[rph], wr=[rS[tc]])

            for pr in range(2):
                n0, n1 = 2 * pr, 2 * pr + 1
                a4_up(n0)
                a4_up(n1)
                a4_sq(n0)
                a4_trans(n0)
                a4_stats(n0)
                a4_sq(n1)
                a4_trans(n1)
                a4_stats(n1)
                a4_h(n0)
                a4_h(n1)
                a4_ffn1(n0)
                a4_ffn2(n0)
                a4_ffn1(n1)
                a4_ffn2(n1)
            P.fence()
            P.replay(blk)
        esA4.close()
    esU.close()

    esM = ExitStack()
    OST = esM.enter_context(nc.sbuf_tensor("OST", [128, 2 * 1024], F32))
    EB2R = esM.enter_context(nc.sbuf_tensor("EB2R", [8, 1024], BF16))
    rOST = [Res("OST0"), Res("OST1")]
    WE1 = esM.enter_context(nc.sbuf_tensor("WE1", [128, 2 * 8 * 2048], BF16))
    WE2 = esM.enter_context(nc.sbuf_tensor("WE2", [128, 16 * 1024], BF16))
    EHS = esM.enter_context(nc.sbuf_tensor("EHS", [128, 16 * 512], BF16))
    PA, PB = Fb(0), Fb(1)
    PE2A = PF[:, 1024:2048]
    PE2B = PF[:, 2048:3072]
    if True:
        rWE1Q = [[Res(f"we1_{sl}_{q}") for q in range(4)] for sl in range(2)]
        rWE2Q = [Res(f"we2_{q}") for q in range(4)]
        rEHSlo, rEHShi = Res("EHSlo"), Res("EHShi")
        rPA, rPB = rPF[0], rPF[1]
        rPE2L = [[rPF[2], rPF[3]], [rPF[4], rPF[5]]]
        PE2L = [PE2A, PE2B]
        WE13 = [WE1[:, 0: 8 * 2048].rearrange("p (k m) -> p k m", k=8),
                WE1[:, 8 * 2048: 16 * 2048].rearrange("p (k m) -> p k m", k=8)]
        WE23 = WE2[:, :].rearrange("p (k m) -> p k m", k=16)
        EHS3 = EHS[:, :].rearrange("p (c x) -> p c x", c=16)
        e1hbm = CN["e1w"].ap().rearrange("e (k p) m -> e p k m", p=128)
        e2hbm = CN["e2w"].ap().rearrange("e (k p) m -> e p k m", p=128)
        pbanks = [(PA, rPA), (PB, rPB)]
        P.op("gpsimd", (lambda: g_.dma_start(EB2R[:, :], CN["eb2"].ap())),
             wr=[rEB2R], dma="dcg2")
        for e in range(8):
            slot = e % 2
            for q in range(4):
                P.op("gpsimd", (lambda e=e, q=q, slot=slot: g_.dma_start(
                    WE13[slot][:, :, 512 * q: 512 * q + 512],
                    e1hbm[e, :, :, 512 * q: 512 * q + 512])),
                    wr=[rWE1Q[slot][q]], dma=f"de1{slot}{q}")
            for q in range(4):
                P.op("gpsimd", (lambda e=e, q=q: g_.dma_start(
                    WE23[:, 4 * q: 4 * q + 4, :],
                    e2hbm[e, :, 4 * q: 4 * q + 4, :])),
                    wr=[rWE2Q[q]], dma=f"de2{q}")
            for nt in range(NNT):
                for m in range(16):
                    ps, rps = pbanks[m % 2]
                    blds = [
                        (lambda m=m, nt=nt, k=k, ps=ps, slot=slot: t_.matmul(
                            ps[:, 0:512],
                            WE13[slot][:, k, 128 * m: 128 * m + 128],
                            XN3[:, k, 512 * nt: 512 * nt + 512],
                            start=(k == 0), stop=(k == 7)))
                        for k in range(8)]
                    P.op("tensor", blds, rd=[rWE1Q[slot][m // 4], rXN4[nt]], wr=[rps])
                    P.op("scalar", (lambda e=e, m=m, ps=ps: s_.activation(
                        EHS3[:, m, 0:512], ps[:, 0:512], AF.Silu,
                        bias=CS["eb1c"][:, 16 * e + m: 16 * e + m + 1])),
                        rd=[rps, rCS["eb1c"]], wr=[rEHSlo if m < 8 else rEHShi])
                for tcl in range(4):
                    tc = 4 * nt + tcl
                    PEx = PE2L[tcl % 2]
                    rPEx = rPE2L[tcl % 2]
                    blds_lo, blds_hi = [], []
                    for h2 in range(2):
                        for k in range(16):
                            (blds_lo if k < 8 else blds_hi).append(
                                lambda k=k, h2=h2, tcl=tcl, e=e, PEx=PEx: t_.matmul(
                                    PEx[0:128, 512 * h2: 512 * h2 + 512],
                                    EHS3[:, k, 128 * tcl: 128 * tcl + 128],
                                    WE23[:, k, 512 * h2: 512 * h2 + 512],
                                    start=(k == 0),
                                    stop=(False if e == 7 else k == 15)))
                        if e == 7:
                            blds_hi.append(lambda h2=h2, tc=tc, PEx=PEx: t_.matmul(
                                PEx[0:128, 512 * h2: 512 * h2 + 512],
                                WFM[0:8, 128 * tc: 128 * tc + 128],
                                EB2R[0:8, 512 * h2: 512 * h2 + 512],
                                start=False, stop=True))
                    rdl_hi = [rEHShi, rWE2Q[2], rWE2Q[3]]
                    if e == 7:
                        rdl_hi += [rWFM, rEB2R]
                    P.op("tensor", blds_lo,
                         rd=[rEHSlo, rWE2Q[0], rWE2Q[1]], wr=rPEx)
                    P.op("tensor", blds_hi, rd=rdl_hi, wr=rPEx)
                    P.op("vector", (lambda e=e, tc=tc, PEx=PEx: v_.scalar_tensor_tensor(
                        S[:, 1024 * tc: 1024 * tc + 1024], PEx[0:128, 0:1024],
                        WTM[:, 8 * tc + e: 8 * tc + e + 1],
                        S[:, 1024 * tc: 1024 * tc + 1024], ALU.mult, ALU.add)),
                        rd=rPEx + [rWTM], wr=[rS[tc]])
                    if e == 7:
                        par = tc % 2
                        P.op("scalar", (lambda tc=tc, par=par: s_.copy(
                            OST[:, 1024 * par: 1024 * par + 1024],
                            S[:, 1024 * tc: 1024 * tc + 1024])),
                            rd=[rS[tc]], wr=[rOST[par]])
                        P.op("sync", (lambda tc=tc, par=par: y_.dma_start(
                            outd.ap()[128 * tc: 128 * tc + 128, :],
                            OST[:, 1024 * par: 1024 * par + 1024])),
                            rd=[rOST[par]], dma=f"do{par}")
        P.fence()
        P.replay(blk)
    esM.close()

    if True:
        P.fence_pend["sync"]["do0"] = P.cnt.get("do0", 0)
        P.fence_pend["sync"]["do1"] = P.cnt.get("do1", 0)
        P.op("sync", (lambda: y_.nop()))
        P.replay(blk)

    blk_cm.__exit__(None, None, None)
    es.close()
    return nc


_NC_CACHE = None


def kernel(**inp):
    global _NC_CACHE
    x = np.ascontiguousarray(np.asarray(inp["L0_attn_x"], np.float32))
    consts = build_consts(inp)
    if _NC_CACHE is None:
        _NC_CACHE = build_nc()
    nc = _NC_CACHE
    in_maps = []
    for c in range(8):
        m = {"x": np.ascontiguousarray(x[c])}
        m.update(consts)
        in_maps.append(m)
    res = run_bass_kernel_spmd(nc, in_maps, core_ids=list(range(8)))
    return np.stack([np.asarray(r["out"], np.float32) for r in res.results], 0)
